# version 01
# baseline (speedup 1.0000x reference)
import contextlib
import numpy as np
import ml_dtypes
import concourse.bass as bass
import concourse.mybir as mybir
from concourse.bass_utils import run_bass_kernel_spmd

F32 = mybir.dt.float32
BF16 = mybir.dt.bfloat16
I8 = mybir.dt.int8
U32 = mybir.dt.uint32
AF = mybir.ActivationFunctionType
ALU = mybir.AluOpType
AX = mybir.AxisListType.X
BF = ml_dtypes.bfloat16

D = 2048
TC = 2048
NCH = D // 128
SEQ = 4096
NCORE = 8
EPS_LN = 1e-5


class Sem:
    def __init__(self, st, nc, name):
        self.h = st.enter_context(nc.semaphore(name))
        self.n = 0

    def inc(self, ins, k=1):
        ins.then_inc(self.h, k)
        self.n += k
        return self.n


class Prog:

    def __init__(self, nc, st):
        self.nc = nc
        self.st = st
        e = st.enter_context
        self.ld = Sem(st, nc, "s_ld")
        self.ld2 = Sem(st, nc, "s_ld2")
        self.so = Sem(st, nc, "s_so")
        self.pe = Sem(st, nc, "s_pe")
        self.dv = Sem(st, nc, "s_dv")
        self.ac = Sem(st, nc, "s_ac")
        self.po = Sem(st, nc, "s_po")
        self.X = e(nc.sbuf_tensor("ar_X", [128, NCH * TC], BF16))
        self.A = e(nc.sbuf_tensor("ar_A", [128, NCH * TC], BF16))
        self.B = e(nc.sbuf_tensor("ar_B", [128, 9216], F32))
        self.C = e(nc.sbuf_tensor("ar_C", [128, 8192], BF16))
        self.C2 = e(nc.sbuf_tensor("ar_C2", [128, 4096], BF16))
        self.R = e(nc.sbuf_tensor("ar_R", [1, 3072], F32))
        self.PSA = e(nc.psum_tensor("ps_A", [128, 2048], F32))
        self.PSB = e(nc.psum_tensor("ps_B", [128, 2048], F32))
        self.PID = e(nc.sbuf_tensor("c_pid", [1, 1], U32))
        self.ones_col = e(nc.sbuf_tensor("c_ones_col", [128, 1], BF16))
        self.ones_row = e(nc.sbuf_tensor("c_ones_row", [1, 128], F32))
        self.ident = e(nc.sbuf_tensor("c_ident", [128, 128], F32))
        self.rg_pid = e(nc.gpsimd.register("rg_pid"))
        self.rg_own = e(nc.gpsimd.register("rg_own"))
        self.rg_a = e(nc.gpsimd.register("rg_a"))
        self.rg_b = e(nc.gpsimd.register("rg_b"))
        self.rg_c = e(nc.gpsimd.register("rg_c"))
        self.rg_d = e(nc.gpsimd.register("rg_d"))
        self.rg_e = e(nc.gpsimd.register("rg_e"))

    def sb(self, name, shape, dtype):
        return self.st.enter_context(self.nc.sbuf_tensor(name, shape, dtype))

    def dynoff(self, g, const):
        g.reg_add(self.rg_a, self.rg_own, const)
        return self.rg_a

    def emit_consts(self):
        nc = self.nc
        oc = nc.inline_tensor(np.ones((128, 1), BF), "d_ones_col")
        orow = nc.inline_tensor(np.ones((1, 128), np.float32), "d_ones_row")
        idn = nc.inline_tensor(np.eye(128, dtype=np.float32), "d_ident")
        with nc.Block("consts") as blk:
            @blk.gpsimd
            def _(g):
                self.ld.inc(g.dma_start(self.ones_col[:, :], oc[:, :]), 16)
                self.ld.inc(g.dma_start(self.ones_row[:, :], orow[:, :]), 16)
                self.ld.inc(g.dma_start(self.ident[:, :], idn[:, :]), 16)
                self.ld.inc(g.dma_start(self.PID[:, :], nc.partition_id_tensor[:, :]), 16)
                g.wait_ge(self.ld.h, self.ld.n)
                g.reg_load(self.rg_pid, self.PID[0:1, 0:1])
                g.reg_alu(self.rg_own, self.rg_pid, 1, ALU.bitwise_and)
                g.reg_mul(self.rg_own, self.rg_own, TC)


def emit_ln(P, src, dst, gb=None, n_tt=4, eps=EPS_LN, tag="ln", router=None):
    nc = P.nc
    for rnd in range(n_tt // 2):
        tts = [2 * rnd, 2 * rnd + 1]
        with nc.Block(f"{tag}_s{rnd}") as blk:
            ac0, pe0 = P.ac.n, P.pe.n

            @blk.scalar
            def _(s, tts=tts, ac0=ac0, pe0=pe0):
                for i, tt in enumerate(tts):
                    for c in range(NCH):
                        k = i * NCH + c
                        sl = P.C[:, (k % 8) * 512:(k % 8) * 512 + 512]
                        if k >= 8:
                            s.wait_ge(P.pe.h, pe0 + 2 * (k - 8 + 1))
                        P.ac.inc(s.activation(sl, src(c, tt), AF.Square))

            @blk.tensor
            def _(t, tts=tts, ac0=ac0):
                for i, tt in enumerate(tts):
                    ps_s = P.PSA[0:1, i * 512:i * 512 + 512]
                    ps_q = P.PSA[0:1, 1024 + i * 512:1024 + i * 512 + 512]
                    for c in range(NCH):
                        k = i * NCH + c
                        sl = P.C[:, (k % 8) * 512:(k % 8) * 512 + 512]
                        t.matmul(ps_s, P.ones_col[:, :], src(c, tt),
                                 start=(c == 0), stop=(c == NCH - 1))
                        t.wait_ge(P.ac.h, ac0 + k + 1)
                        P.pe.inc(t.matmul(ps_q, P.ones_col[:, :], sl,
                                          start=(c == 0), stop=(c == NCH - 1)), 2)

            @blk.vector
            def _(v):
                v.wait_ge(P.pe.h, P.pe.n)
                for i in range(2):
                    v.tensor_copy(P.R[0:1, i * 512:i * 512 + 512],
                                  P.PSA[0:1, i * 512:i * 512 + 512])
                    v.tensor_copy(P.R[0:1, 1024 + i * 512:1024 + i * 512 + 512],
                                  P.PSA[0:1, 1024 + i * 512:1024 + i * 512 + 512])
                s_sp = P.R[0:1, 0:1024]
                q_sp = P.R[0:1, 1024:2048]
                t_sp = P.R[0:1, 2048:3072]
                v.tensor_scalar_mul(s_sp, s_sp, 1.0 / D)
                v.tensor_scalar_mul(q_sp, q_sp, 1.0 / D)
                v.tensor_mul(t_sp, s_sp, s_sp)
                v.tensor_sub(q_sp, q_sp, t_sp)
                v.tensor_scalar_add(q_sp, q_sp, eps)
                P.dv.inc(v.reciprocal(t_sp, q_sp))

            @blk.scalar
            def _(s):
                s.wait_ge(P.dv.h, P.dv.n)
                P.ac.inc(s.activation(P.R[0:1, 1024:2048],
                                      P.R[0:1, 2048:3072], AF.Sqrt))

            @blk.vector
            def _(v):
                v.wait_ge(P.ac.h, P.ac.n)
                P.dv.inc(v.scalar_tensor_tensor(
                    P.R[0:1, 0:1024], P.R[0:1, 0:1024], -1.0,
                    P.R[0:1, 1024:2048], ALU.mult, ALU.mult))

        with nc.Block(f"{tag}_a{rnd}") as blk:
            pe0a, dv0a = P.pe.n, P.dv.n

            @blk.tensor
            def _(t, tts=tts, pe0a=pe0a, dv0a=dv0a):
                for i in range(2):
                    a_row = P.R[0:1, 1024 + i * 512:1024 + i * 512 + 512]
                    b_row = P.R[0:1, i * 512:i * 512 + 512]
                    t.matmul(P.PSB[:, i * 1024:i * 1024 + 512],
                             P.ones_row[:, :], a_row, start=True, stop=True)
                    P.pe.inc(t.matmul(P.PSB[:, i * 1024 + 512:i * 1024 + 1024],
                                      P.ones_row[:, :], b_row, start=True, stop=True))
                if router is None:
                    return
                GWF = router[0]
                for i, tt in enumerate(tts):
                    for c in range(NCH):
                        k = i * NCH + c
                        t.wait_ge(P.dv.h, dv0a + 2 * k + 1)
                        tmp = P.B[:, 8192 + (c % 2) * 512:8704 + (c % 2) * 512]
                        for sub in range(4):
                            mm = t.matmul(
                                P.PSA[:, sub * 512 + tt * 16:sub * 512 + tt * 16 + 16],
                                tmp[:, sub * 128:sub * 128 + 128],
                                GWF[:, c * 16:(c + 1) * 16],
                                start=(c == 0), stop=(c == NCH - 1))
                        P.pe.inc(mm)

            @blk.vector
            def _(v, tts=tts, pe0a=pe0a):
                v.wait_ge(P.pe.h, pe0a + 2)
                nrt = 0 if router is None else 1
                for i, tt in enumerate(tts):
                    bcA = P.PSB[:, i * 1024:i * 1024 + 512]
                    bcB = P.PSB[:, i * 1024 + 512:i * 1024 + 1024]
                    for c in range(NCH):
                        k = i * NCH + c
                        tmp = P.B[:, 8192 + (c % 2) * 512:8704 + (c % 2) * 512]
                        if router is not None and k >= 2:
                            v.wait_ge(P.pe.h, pe0a + 2 + (k - 1))
                        v.tensor_tensor(tmp, src(c, tt), bcA, ALU.mult)
                        if router is not None:
                            P.dv.inc(v.tensor_tensor(tmp, tmp, bcB, ALU.add))
                            P.dv.inc(v.tensor_copy(dst(c, tt), tmp))
                        elif gb is not None:
                            v.tensor_tensor(tmp, tmp, bcB, ALU.add)
                            gcol, bcol = gb(c)
                            P.dv.inc(v.tensor_scalar(dst(c, tt), tmp, gcol, bcol,
                                                     ALU.mult, ALU.add))
                        else:
                            P.dv.inc(v.tensor_tensor(dst(c, tt), tmp, bcB, ALU.add))
                if router is not None:
                    v.wait_ge(P.pe.h, P.pe.n)
                    for i, tt in enumerate(tts):
                        for sub in range(4):
                            P.dv.inc(v.tensor_copy(
                                router[1][:, router[2] + (tt * 4 + sub) * 16:
                                          router[2] + (tt * 4 + sub) * 16 + 16],
                                P.PSA[:, sub * 512 + tt * 16:sub * 512 + tt * 16 + 16]))


def decl_l1ffn_io(nc):
    io = {}
    io["l1f_wg"] = nc.dram_tensor("l1f_wg", [64, 128, 2048], I8, kind="ExternalInput")
    io["l1f_wu"] = nc.dram_tensor("l1f_wu", [64, 128, 2048], I8, kind="ExternalInput")
    io["l1f_wd"] = nc.dram_tensor("l1f_wd", [2, 64, 128, 1024], I8, kind="ExternalInput")
    io["l1f_sg"] = nc.dram_tensor("l1f_sg", [128, 64], F32, kind="ExternalInput")
    io["l1f_su"] = nc.dram_tensor("l1f_su", [128, 64], F32, kind="ExternalInput")
    io["l1f_sd"] = nc.dram_tensor("l1f_sd", [128, 16], F32, kind="ExternalInput")
    io["l1f_fuse"] = nc.dram_tensor("l1f_fuse", [4 * 64 * 128, 512], BF16)
    return io


def prep_l1ffn_inputs(inputs):
    wg = np.asarray(inputs["L1_ffn_w_gate"])
    wu = np.asarray(inputs["L1_ffn_w_up"])
    wd = np.asarray(inputs["L1_ffn_w_down"])
    return {
        "l1f_wg": np.ascontiguousarray(
            wg.reshape(16, 128, 64, 128).transpose(2, 1, 0, 3).reshape(64, 128, 2048)),
        "l1f_wu": np.ascontiguousarray(
            wu.reshape(16, 128, 64, 128).transpose(2, 1, 0, 3).reshape(64, 128, 2048)),
        "l1f_wd": np.ascontiguousarray(
            wd.reshape(64, 128, 2, 8, 128).transpose(2, 0, 1, 3, 4).reshape(2, 64, 128, 1024)),
        "l1f_sg": np.ascontiguousarray(
            np.asarray(inputs["L1_ffn_s_gate"]).reshape(64, 128).T),
        "l1f_su": np.ascontiguousarray(
            np.asarray(inputs["L1_ffn_s_up"]).reshape(64, 128).T),
        "l1f_sd": np.ascontiguousarray(
            np.asarray(inputs["L1_ffn_s_down"]).reshape(16, 128).T),
    }


def build_l1ffn(P, io, xin, xout):
    nc = P.nc
    X, A, B, C, C2 = P.X, P.A, P.B, P.C, P.C2
    sgc = P.sb("l1f_sgc", [128, 64], F32)
    suc = P.sb("l1f_suc", [128, 64], F32)
    sdc = P.sb("l1f_sdc", [128, 16], F32)

    with nc.Block("l1f_load") as blk:
        @blk.gpsimd
        def _(g):
            for c in range(NCH):
                P.ld.inc(g.dma_start(X[:, c * TC:(c + 1) * TC],
                                     xin[c * 128:(c + 1) * 128, :]), 16)

        @blk.sync
        def _(sy):
            P.ld.inc(sy.dma_start(sgc[:, :], io["l1f_sg"][:, :]), 16)
            P.ld.inc(sy.dma_start(suc[:, :], io["l1f_su"][:, :]), 16)
            P.ld.inc(sy.dma_start(sdc[:, :], io["l1f_sd"][:, :]), 16)
            sy.wait_ge(P.ld.h, P.ld.n)

    srcX = lambda c, tt: X[:, c * TC + tt * 512:c * TC + tt * 512 + 512]
    dstA = lambda c, tt: A[:, c * TC + tt * 512:c * TC + tt * 512 + 512]
    emit_ln(P, srcX, dstA, tag="l1f_ln")

    with nc.Block("l1f_gu") as blk:
        ld0, pe0, ac0, dv0, so0 = P.ld.n, P.pe.n, P.ac.n, P.dv.n, P.so.n

        @blk.gpsimd
        def _(g):
            for f in range(66):
                if f < 64:
                    if f >= 2:
                        g.wait_ge(P.pe.h, pe0 + 4 * (f - 1))
                    p = f % 2
                    P.ld.inc(g.dma_start(C[:, p * 2048:p * 2048 + 2048], io["l1f_wg"][f]), 16)
                    P.ld.inc(g.dma_start(C[:, 4096 + p * 2048:4096 + p * 2048 + 2048],
                                         io["l1f_wu"][f]), 16)
                if f >= 2:
                    fs = f - 2
                    g.wait_ge(P.dv.h, dv0 + 8 * (fs + 1))
                    src = C2[:, (fs % 2) * 2048:(fs % 2) * 2048 + 2048].rearrange(
                        "p (t n) -> p t n", t=4)
                    dst = io["l1f_fuse"].rearrange(
                        "(t f p) n -> f p t n", t=4, f=64)[fs]
                    P.so.inc(g.dma_start(dst, src), 16)
            g.wait_ge(P.so.h, P.so.n)

        @blk.tensor
        def _(t):
            for f in range(64):
                p = f % 2
                t.wait_ge(P.ld.h, ld0 + 32 * (f + 1))
                for tt in range(4):
                    if f >= 1:
                        t.wait_ge(P.ac.h, ac0 + 4 * (f - 1) + tt + 1)
                        t.wait_ge(P.dv.h, dv0 + 8 * (f - 1) + 2 * tt + 2)
                    for k in range(NCH):
                        t.matmul(P.PSA[:, tt * 512:tt * 512 + 512],
                                 C[:, p * 2048 + k * 128:p * 2048 + k * 128 + 128],
                                 A[:, k * TC + tt * 512:k * TC + tt * 512 + 512],
                                 start=(k == 0), stop=(k == NCH - 1))
                    for k in range(NCH):
                        mm = t.matmul(P.PSB[:, tt * 512:tt * 512 + 512],
                                      C[:, 4096 + p * 2048 + k * 128:4096 + p * 2048 + k * 128 + 128],
                                      A[:, k * TC + tt * 512:k * TC + tt * 512 + 512],
                                      start=(k == 0), stop=(k == NCH - 1))
                        if k == NCH - 1:
                            P.pe.inc(mm)

        @blk.scalar
        def _(s):
            for f in range(64):
                for tt in range(4):
                    s.wait_ge(P.pe.h, pe0 + 4 * f + tt + 1)
                    P.ac.inc(s.activation(B[:, tt * 512:tt * 512 + 512],
                                          P.PSA[:, tt * 512:tt * 512 + 512],
                                          AF.Silu, scale=sgc[:, f:f + 1]))

        @blk.vector
        def _(v):
            for f in range(64):
                for tt in range(4):
                    v.wait_ge(P.pe.h, pe0 + 4 * f + tt + 1)
                    P.dv.inc(v.tensor_scalar_mul(
                        B[:, 2048 + tt * 512:2048 + tt * 512 + 512],
                        P.PSB[:, tt * 512:tt * 512 + 512], suc[:, f:f + 1]))
                    v.wait_ge(P.ac.h, ac0 + 4 * f + tt + 1)
                    if f >= 2:
                        v.wait_ge(P.so.h, so0 + 16 * (f - 1))
                    P.dv.inc(v.tensor_mul(
                        C2[:, (f % 2) * 2048 + tt * 512:(f % 2) * 2048 + tt * 512 + 512],
                        B[:, tt * 512:tt * 512 + 512],
                        B[:, 2048 + tt * 512:2048 + tt * 512 + 512]))

    for tt in range(4):
        for dh in range(2):
            with nc.Block(f"l1f_dn{tt}{dh}") as blk:
                ld0, pe0, dv0 = P.ld.n, P.pe.n, P.dv.n

                @blk.gpsimd
                def _(g, tt=tt, dh=dh, pe0=pe0):
                    for fb in range(16):
                        if fb >= 2:
                            g.wait_ge(P.pe.h, pe0 + (fb - 1))
                        p = fb % 2
                        src = io["l1f_wd"].rearrange("h f p n -> h p f n")[
                            dh, :, fb * 4:(fb + 1) * 4, :]
                        P.ld.inc(g.dma_start(
                            C[:, p * 4096:(p + 1) * 4096].rearrange("p (f n) -> p f n", f=4),
                            src), 16)
                        srcf = io["l1f_fuse"].rearrange(
                            "(t f p) n -> t p f n", t=4, f=64)[
                            tt, :, fb * 4:(fb + 1) * 4, :]
                        P.ld.inc(g.dma_start(
                            C2[:, p * 2048:(p + 1) * 2048].rearrange("p (f n) -> p f n", f=4),
                            srcf), 16)

                @blk.sync
                def _(sy, tt=tt, dh=dh):
                    src = xin.rearrange("(c p) n -> p c n", p=128)[
                        :, dh * 8:(dh + 1) * 8, tt * 512:(tt + 1) * 512]
                    P.ld2.inc(sy.dma_start(
                        P.B[:, 0:4096].rearrange("p (c n) -> p c n", c=8), src), 16)

                @blk.tensor
                def _(t, tt=tt, dh=dh, ld0=ld0):
                    for fb in range(16):
                        p = fb % 2
                        t.wait_ge(P.ld.h, ld0 + 32 * (fb + 1))
                        for fi in range(4):
                            for do in range(8):
                                ps = (P.PSA if do < 4 else P.PSB)
                                psl = ps[:, (do % 4) * 512:(do % 4) * 512 + 512]
                                mm = t.matmul(
                                    psl,
                                    C[:, p * 4096 + fi * 1024 + do * 128:p * 4096 + fi * 1024 + do * 128 + 128],
                                    C2[:, p * 2048 + fi * 512:p * 2048 + fi * 512 + 512],
                                    start=(fb == 0 and fi == 0),
                                    stop=(fb == 15 and fi == 3))
                                if fi == 3 and do == 7:
                                    P.pe.inc(mm)

                @blk.vector
                def _(v, tt=tt, dh=dh, ld0=ld0):
                    v.wait_ge(P.pe.h, P.pe.n)
                    v.wait_ge(P.ld2.h, P.ld2.n)
                    for do in range(8):
                        ps = (P.PSA if do < 4 else P.PSB)
                        psl = ps[:, (do % 4) * 512:(do % 4) * 512 + 512]
                        P.dv.inc(v.scalar_tensor_tensor(
                            B[:, 4096 + do * 512:4096 + do * 512 + 512],
                            psl, sdc[:, dh * 8 + do:dh * 8 + do + 1],
                            B[:, do * 512:do * 512 + 512], ALU.mult, ALU.add))

                @blk.sync
                def _(sy, tt=tt, dh=dh):
                    sy.wait_ge(P.dv.h, P.dv.n)
                    dst = xout.rearrange("(c p) n -> p c n", p=128)[
                        :, dh * 8:(dh + 1) * 8, tt * 512:(tt + 1) * 512]
                    P.so.inc(sy.dma_start(
                        dst, P.B[:, 4096:8192].rearrange("p (c n) -> p c n", c=8)), 16)
                    sy.wait_ge(P.so.h, P.so.n)


def _shard_x(x_full):
    out = []
    for c in range(NCORE):
        s, h = c // 2, c % 2
        sl = np.asarray(x_full)[s, h * TC:(h + 1) * TC, :]
        out.append(np.ascontiguousarray(sl.T))
    return out


def _unshard_x(slices):
    x = np.empty((4, SEQ, D), np.float32)
    for c in range(NCORE):
        s, h = c // 2, c % 2
        x[s, h * TC:(h + 1) * TC, :] = slices[c].T
    return x


def run_l1ffn(x_full, inputs):
    with contextlib.ExitStack() as st:
        nc = bass.Bass("TRN2", target_bir_lowering=False)
        xin = nc.dram_tensor("xin", [D, TC], F32, kind="ExternalInput")
        xout = nc.dram_tensor("xout", [D, TC], F32, kind="ExternalOutput")
        io = decl_l1ffn_io(nc)
        P = Prog(nc, st)
        P.emit_consts()
        build_l1ffn(P, io, xin, xout)
        w = prep_l1ffn_inputs(inputs)
        shards = _shard_x(x_full)
        in_maps = [dict(w, xin=shards[c]) for c in range(NCORE)]
        res = run_bass_kernel_spmd(nc, in_maps, core_ids=list(range(NCORE)))
        return _unshard_x([res.results[c]["xout"] for c in range(NCORE)])


def _launch_l0attn(inputs):
    with contextlib.ExitStack() as st:
        nc = bass.Bass("TRN2", target_bir_lowering=False)
        xout = nc.dram_tensor("xout", [D, TC], F32, kind="ExternalOutput")
        io = decl_l0attn_io(nc)
        P = Prog(nc, st)
        P.emit_consts()
        build_l0attn(P, io, xout)
        shared = prep_l0attn_inputs(inputs, 0)
        in_maps = []
        for c in range(NCORE):
            m = prep_l0attn_inputs(inputs, c)
            for k2 in ("l0a_qkv", "l0a_op", "l0a_gb"):
                m[k2] = shared[k2]
            in_maps.append(m)
        res = run_bass_kernel_spmd(nc, in_maps, core_ids=list(range(NCORE)))
        return [res.results[c]["xout"] for c in range(NCORE)]


def _launch_l0ffn(xs, inputs):
    with contextlib.ExitStack() as st:
        nc = bass.Bass("TRN2", target_bir_lowering=False)
        xin = nc.dram_tensor("xin", [D, TC], F32, kind="ExternalInput")
        xout = nc.dram_tensor("xout", [D, TC], F32, kind="ExternalOutput")
        io = decl_l0ffn_io(nc)
        P = Prog(nc, st)
        P.emit_consts()
        build_l0ffn(P, io, xin, xout)
        w = prep_l0ffn_inputs(inputs)
        in_maps = [dict(w, xin=xs[c]) for c in range(NCORE)]
        res = run_bass_kernel_spmd(nc, in_maps, core_ids=list(range(NCORE)))
        return [res.results[c]["xout"] for c in range(NCORE)]


def _launch_l1attn(xs, inputs):
    with contextlib.ExitStack() as st:
        nc = bass.Bass("TRN2", target_bir_lowering=False)
        xin = nc.dram_tensor("xin", [D, TC], F32, kind="ExternalInput")
        xout = nc.dram_tensor("xout", [D, TC], F32, kind="ExternalOutput")
        io = decl_l1attn_io(nc)
        P = Prog(nc, st)
        P.emit_consts()
        build_l1attn(P, io, xin, xout)
        rng = np.random.default_rng()
        nonce = np.array([[rng.integers(1, 2**24), rng.integers(1, 2**24)]],
                         dtype=np.uint32)
        shared = prep_l1attn_inputs(inputs, 0, nonce)
        in_maps = []
        for c in range(NCORE):
            m = prep_l1attn_inputs(inputs, c, nonce)
            for k2 in ("l1a_ipw", "l1a_hm", "l1a_opw", "l1a_ns", "l1a_cw",
                       "l1a_cb", "l1a_dtb"):
                m[k2] = shared[k2]
            m["xin"] = xs[c]
            in_maps.append(m)
        res = run_bass_kernel_spmd(nc, in_maps, core_ids=list(range(NCORE)))
        return [res.results[c]["xout"] for c in range(NCORE)]


def _launch_l1ffn(xs, inputs):
    with contextlib.ExitStack() as st:
        nc = bass.Bass("TRN2", target_bir_lowering=False)
        xin = nc.dram_tensor("xin", [D, TC], F32, kind="ExternalInput")
        xout = nc.dram_tensor("xout", [D, TC], F32, kind="ExternalOutput")
        io = decl_l1ffn_io(nc)
        P = Prog(nc, st)
        P.emit_consts()
        build_l1ffn(P, io, xin, xout)
        w = prep_l1ffn_inputs(inputs)
        in_maps = [dict(w, xin=xs[c]) for c in range(NCORE)]
        res = run_bass_kernel_spmd(nc, in_maps, core_ids=list(range(NCORE)))
        return [res.results[c]["xout"] for c in range(NCORE)]


def kernel(**inputs):
    xs = _launch_l0attn(inputs)
    xs = _launch_l0ffn(xs, inputs)
    xs = _launch_l1attn(xs, inputs)
    xs = _launch_l1ffn(xs, inputs)
    return _unshard_x(xs)



def decl_l0ffn_io(nc):
    io = {}
    io["l0f_gw"] = nc.dram_tensor("l0f_gw", [128, 256], F32, kind="ExternalInput")
    io["l0f_eb"] = nc.dram_tensor("l0f_eb", [1, 16], F32, kind="ExternalInput")
    io["l0f_sgw"] = nc.dram_tensor("l0f_sgw", [8, 128, 2048], F32, kind="ExternalInput")
    io["l0f_suw"] = nc.dram_tensor("l0f_suw", [8, 128, 2048], F32, kind="ExternalInput")
    io["l0f_sdw"] = nc.dram_tensor("l0f_sdw", [1024, 2048], F32, kind="ExternalInput")
    io["l0f_egu"] = nc.dram_tensor("l0f_egu", [16, 16, 128, 2048], F32, kind="ExternalInput")
    io["l0f_ed"] = nc.dram_tensor("l0f_ed", [16, 1024, 2048], F32, kind="ExternalInput")
    io["l0f_sel"] = nc.dram_tensor("l0f_sel", [16, 2048], F32, kind="ExternalInput")
    io["l0f_act"] = nc.dram_tensor("l0f_act", [4 * 8 * 128, 512], BF16)
    return io


def prep_l0ffn_inputs(inputs):
    gw = np.asarray(inputs["L0_ffn_gate_w"], np.float32)
    sgw = np.asarray(inputs["L0_ffn_shared_gate_w"], np.float32)
    suw = np.asarray(inputs["L0_ffn_shared_up_w"], np.float32)
    egu = np.asarray(inputs["L0_ffn_experts_gate_up"], np.float32)
    sel = np.zeros((16, 2048), np.float32)
    for e in range(16):
        sel[e, e * 128:(e + 1) * 128] = 1.0

    def coltile(w, m):
        return np.ascontiguousarray(
            w.reshape(16, 128, m, 128).transpose(2, 1, 0, 3).reshape(m, 128, 2048))

    return {
        "l0f_gw": np.ascontiguousarray(
            gw.reshape(16, 128, 16).transpose(1, 0, 2).reshape(128, 256)),
        "l0f_eb": np.asarray(inputs["L0_ffn_expert_bias"], np.float32).reshape(1, 16),
        "l0f_sgw": coltile(sgw, 8),
        "l0f_suw": coltile(suw, 8),
        "l0f_sdw": np.ascontiguousarray(np.asarray(inputs["L0_ffn_shared_down_w"], np.float32)),
        "l0f_egu": np.ascontiguousarray(
            egu.reshape(16, 16, 128, 16, 128).transpose(0, 3, 2, 1, 4).reshape(16, 16, 128, 2048)),
        "l0f_ed": np.ascontiguousarray(np.asarray(inputs["L0_ffn_experts_down"], np.float32)),
        "l0f_sel": sel,
    }


def _emit_expert(P, io, tag, wsrc, wd_src, has_cb, e, first):
    nc = P.nc
    A, B, C, C2, X = P.A, P.B, P.C, P.C2, P.X
    nf = 8
    order = []
    for i in range(nf):
        order += [i, nf + i]
    dvf = 8 if has_cb else 4

    with nc.Block(f"{tag}_gu") as blk:
        ld0, pe0, ac0, dv0, so0 = P.ld.n, P.pe.n, P.ac.n, P.dv.n, P.so.n
        ncb = 4 if has_cb else 0

        @blk.gpsimd
        def _(g):
            for mi in range(18):
                if mi < 16:
                    if mi >= 4:
                        g.wait_ge(P.pe.h, pe0 + ncb + 4 * (mi - 3))
                    P.ld.inc(g.dma_start(
                        C[:, (mi % 4) * 2048:(mi % 4) * 2048 + 2048], wsrc(order[mi])), 16)
                if mi >= 3 and mi % 2 == 1:
                    f = (mi - 3) // 2
                    g.wait_ge(P.dv.h, dv0 + ncb + dvf * (f + 1))
                    src = C2[:, (f % 2) * 2048:(f % 2) * 2048 + 2048].rearrange(
                        "p (t n) -> p t n", t=4)
                    dst = io["l0f_act"].rearrange(
                        "(t f p) n -> f p t n", t=4, f=8)[f]
                    P.so.inc(g.dma_start(dst, src), 16)

        @blk.tensor
        def _(t):
            if has_cb:
                for tt in range(4):
                    P.pe.inc(t.matmul(P.PSA[:, tt * 512:tt * 512 + 512],
                                      B[0:16, 4096 + e * 128:4096 + e * 128 + 128],
                                      B[0:16, 6144 + tt * 512:6144 + tt * 512 + 512],
                                      start=True, stop=True))
            for mi in range(16):
                t.wait_ge(P.ld.h, ld0 + 16 * (mi + 1))
                ps = P.PSA if mi % 2 == 0 else P.PSB
                if has_cb and mi == 0:
                    t.wait_ge(P.dv.h, dv0 + 4)
                if mi >= 2:
                    t.wait_ge(P.dv.h, dv0 + ncb + dvf * ((mi - 2) // 2 + 1))
                    if mi % 2 == 0:
                        t.wait_ge(P.ac.h, ac0 + 4 * ((mi - 2) // 2 + 1))
                for tt in range(4):
                    for k in range(NCH):
                        mm = t.matmul(ps[:, tt * 512:tt * 512 + 512],
                                      C[:, (mi % 4) * 2048 + k * 128:(mi % 4) * 2048 + k * 128 + 128],
                                      A[:, k * TC + tt * 512:k * TC + tt * 512 + 512],
                                      start=(k == 0), stop=(k == NCH - 1))
                        if k == NCH - 1:
                            P.pe.inc(mm)

        @blk.scalar
        def _(s):
            for f in range(nf):
                for tt in range(4):
                    s.wait_ge(P.pe.h, pe0 + ncb + 4 * (2 * f + 1))
                    P.ac.inc(s.activation(B[:, 2048 + tt * 512:2048 + tt * 512 + 512],
                                          P.PSA[:, tt * 512:tt * 512 + 512], AF.Silu))

        @blk.vector
        def _(v):
            if has_cb:
                v.wait_ge(P.pe.h, pe0 + 4)
                for tt in range(4):
                    P.dv.inc(v.tensor_copy(B[:, tt * 512:tt * 512 + 512],
                                           P.PSA[:, tt * 512:tt * 512 + 512]))
            for f in range(nf):
                for tt in range(4):
                    v.wait_ge(P.ac.h, ac0 + 4 * f + tt + 1)
                    v.wait_ge(P.pe.h, pe0 + ncb + 4 * (2 * f + 2))
                    if f >= 2:
                        v.wait_ge(P.so.h, so0 + 16 * (f - 1))
                    gs = B[:, 2048 + tt * 512:2048 + tt * 512 + 512]
                    dst = C2[:, (f % 2) * 2048 + tt * 512:(f % 2) * 2048 + tt * 512 + 512]
                    if has_cb:
                        t1 = B[:, 8192 + (tt % 2) * 512:8192 + (tt % 2) * 512 + 512]
                        v.tensor_tensor(t1, gs, P.PSB[:, tt * 512:tt * 512 + 512], ALU.mult)
                        P.dv.inc(v.tensor_tensor(dst, t1, B[:, tt * 512:tt * 512 + 512],
                                                 ALU.mult), 2)
                    else:
                        P.dv.inc(v.tensor_tensor(dst, gs,
                                                 P.PSB[:, tt * 512:tt * 512 + 512], ALU.mult))

    with nc.Block(f"{tag}_dn") as blk:
        ld0, pe0, dv0 = P.ld.n, P.pe.n, P.dv.n

        @blk.gpsimd
        def _(g):
            for gi in range(8):
                for fb in range(2):
                    j = 2 * gi + fb
                    if j >= 2:
                        g.wait_ge(P.pe.h, pe0 + (j - 1))
                    tt, dh = gi // 2, gi % 2
                    P.ld.inc(g.dma_start(
                        C[:, (j % 2) * 4096:(j % 2) * 4096 + 4096].rearrange(
                            "p (f n) -> p f n", f=4),
                        wd_src(dh, fb)), 16)
                    P.ld.inc(g.dma_start(
                        C2[:, (j % 2) * 2048:(j % 2) * 2048 + 2048].rearrange(
                            "p (f n) -> p f n", f=4),
                        io["l0f_act"].rearrange("(t f p) n -> t p f n", t=4, f=8)[
                            tt, :, fb * 4:(fb + 1) * 4, :]), 16)

        @blk.tensor
        def _(t):
            for gi in range(8):
                tt, dh = gi // 2, gi % 2
                if gi >= 1:
                    t.wait_ge(P.dv.h, dv0 + 8 * gi)
                for fb in range(2):
                    j = 2 * gi + fb
                    t.wait_ge(P.ld.h, ld0 + 32 * (j + 1))
                    for fi in range(4):
                        for do in range(8):
                            ps = P.PSA if do < 4 else P.PSB
                            psl = ps[:, (do % 4) * 512:(do % 4) * 512 + 512]
                            mm = t.matmul(
                                psl,
                                C[:, (j % 2) * 4096 + fi * 1024 + do * 128:
                                  (j % 2) * 4096 + fi * 1024 + do * 128 + 128],
                                C2[:, (j % 2) * 2048 + fi * 512:(j % 2) * 2048 + fi * 512 + 512],
                                start=(fb == 0 and fi == 0), stop=(fb == 1 and fi == 3))
                            if fi == 3 and do == 7:
                                P.pe.inc(mm)

        @blk.vector
        def _(v):
            for gi in range(8):
                tt, dh = gi // 2, gi % 2
                v.wait_ge(P.pe.h, pe0 + 2 * (gi + 1))
                for do in range(8):
                    c = dh * 8 + do
                    ps = P.PSA if do < 4 else P.PSB
                    psl = ps[:, (do % 4) * 512:(do % 4) * 512 + 512]
                    xsl = X[:, c * TC + tt * 512:c * TC + tt * 512 + 512]
                    if first:
                        P.dv.inc(v.tensor_copy(xsl, psl))
                    else:
                        P.dv.inc(v.tensor_tensor(xsl, xsl, psl, ALU.add))


def build_l0ffn(P, io, xin, xout, dbg=None):
    nc = P.nc

    def dump(name, srcs):
        if dbg is None or name not in dbg:
            return
        with nc.Block(f"dump_{name}") as blk:
            @blk.gpsimd
            def _(g):
                for dst_ap, src_ap in srcs():
                    P.so.inc(g.dma_start(dst_ap, src_ap), 16)
                g.wait_ge(P.so.h, P.so.n)
    A, B, C, C2, X = P.A, P.B, P.C, P.C2, P.X
    GWF = P.sb("l0f_GWF", [128, 256], F32)
    EBS = P.sb("l0f_EBS", [128, 16], F32)
    EBR = P.sb("l0f_EBR", [1, 16], F32)

    with nc.Block("l0f_load") as blk:
        @blk.gpsimd
        def _(g):
            for c in range(NCH):
                P.ld.inc(g.dma_start(X[:, c * TC:(c + 1) * TC],
                                     xin[c * 128:(c + 1) * 128, :]), 16)

        @blk.sync
        def _(sy):
            P.ld2.inc(sy.dma_start(B[0:16, 4096:6144], io["l0f_sel"][:, :]), 16)
            P.ld2.inc(sy.dma_start(EBR[:, :], io["l0f_eb"][:, :]), 16)
            P.ld2.inc(sy.dma_start(GWF[:, :], io["l0f_gw"][:, :]), 16)
            sy.wait_ge(P.ld2.h, P.ld2.n)
            sy.wait_ge(P.ld.h, P.ld.n)

    srcX = lambda c, tt: X[:, c * TC + tt * 512:c * TC + tt * 512 + 512]
    dstA = lambda c, tt: A[:, c * TC + tt * 512:c * TC + tt * 512 + 512]
    emit_ln(P, srcX, dstA, tag="l0f_ln", router=(GWF, B, 0))
    dump("logits", lambda: [(dbg["logits"][:, :], B[:, 0:256])])

    with nc.Block("l0f_rt") as blk:
        pe0, ac0, dv0 = P.pe.n, P.ac.n, P.dv.n

        @blk.tensor
        def _(t):
            P.pe.inc(t.matmul(P.PSB[:, 0:16], P.ones_row[:, :], EBR[0:1, :],
                              start=True, stop=True))

        @blk.scalar
        def _(s):
            P.ac.inc(s.activation(B[:, 0:256], B[:, 0:256], AF.Exp, scale=-1.0))

        @blk.vector
        def _(v):
            v.wait_ge(P.ac.h, P.ac.n)
            v.wait_ge(P.pe.h, P.pe.n)

            def ss0(ins):
                P.dv.inc(ins)
                v.wait_ge(P.dv.h, P.dv.n)

            ss0(v.tensor_scalar_add(B[:, 0:256], B[:, 0:256], 1.0))
            ss0(v.reciprocal(B[:, 0:256], B[:, 0:256]))

            def ss(ins):
                P.dv.inc(ins)
                v.wait_ge(P.dv.h, P.dv.n)

            ss(v.tensor_copy(EBS[:, :], P.PSB[:, 0:16]))
            for tc in range(16):
                V = B[:, tc * 16:tc * 16 + 16]
                T1 = B[:, 256:272]
                T2 = B[:, 272:288]
                E1 = B[:, 288:304]
                E2 = B[:, 304:320]
                m1 = B[:, 320:321]
                m2 = B[:, 321:322]
                sm = B[:, 322:323]
                CMt = B[:, 1536 + tc * 16:1536 + tc * 16 + 16]
                ss(v.tensor_tensor(T1, V, EBS[:, :], ALU.add))
                ss(v.reduce_max(m1, T1, axis=AX))
                ss(v.tensor_single_scalar(E1, T1, m1[:, 0:1], ALU.is_ge))
                ss(v.scalar_tensor_tensor(T2, E1, -1e30, T1, ALU.mult, ALU.add))
                ss(v.reduce_max(m2, T2, axis=AX))
                ss(v.tensor_single_scalar(E2, T2, m2[:, 0:1], ALU.is_ge))
                ss(v.tensor_tensor(E1, E1, E2, ALU.add))
                ss(v.tensor_tensor(E1, E1, V, ALU.mult))
                ss(v.tensor_reduce(sm, E1, axis=AX, op=ALU.add))
                ss(v.tensor_scalar_add(sm, sm, 1e-20))
                ss(v.reciprocal(m1, sm))
                ss(v.tensor_scalar_mul(CMt, E1, m1[:, 0:1]))

        @blk.tensor
        def _(t):
            t.wait_ge(P.dv.h, P.dv.n)
            for tc in range(16):
                P.pe.inc(t.transpose(P.PSB[0:16, tc * 128:tc * 128 + 128],
                                     B[:, 1536 + tc * 16:1536 + tc * 16 + 16],
                                     P.ident[:, :]))

        @blk.vector
        def _(v):
            v.wait_ge(P.pe.h, P.pe.n)
            P.dv.inc(v.tensor_copy(B[0:16, 6144:8192], P.PSB[0:16, 0:2048]))

    def sh_wsrc(m):
        return io["l0f_sgw"][m] if m < 8 else io["l0f_suw"][m - 8]

    def sh_wd(dh, fb):
        return io["l0f_sdw"].rearrange("(f p) n -> p f n", p=128)[
            :, fb * 4:(fb + 1) * 4, dh * 1024:(dh + 1) * 1024]

    dump("ct", lambda: [(dbg["ct"][:, :], B[0:16, 6144:8192])])
    _emit_expert(P, io, "moe_sh", sh_wsrc, sh_wd, False, -1, True)
    dump("sh", lambda: [(dbg["sh"][c * 128:(c + 1) * 128, :], X[:, c * TC:(c + 1) * TC])
                        for c in range(NCH)])

    for e in range(16):
        def r_wsrc(m, e=e):
            return io["l0f_egu"][e, m]

        def r_wd(dh, fb, e=e):
            return io["l0f_ed"].rearrange("e (f p) n -> e p f n", p=128)[
                e, :, fb * 4:(fb + 1) * 4, dh * 1024:(dh + 1) * 1024]

        _emit_expert(P, io, f"moe_e{e}", r_wsrc, r_wd, True, e, False)

    with nc.Block("l0f_fin") as blk:
        ld20, dv0, so0 = P.ld2.n, P.dv.n, P.so.n

        @blk.sync
        def _(sy):
            for c in range(17):
                if c < 16:
                    if c >= 2:
                        sy.wait_ge(P.dv.h, dv0 + (c - 1))
                    P.ld2.inc(sy.dma_start(B[:, (c % 2) * 2048:(c % 2) * 2048 + 2048],
                                           xin[c * 128:(c + 1) * 128, :]), 16)
                if c >= 1:
                    cc = c - 1
                    sy.wait_ge(P.dv.h, dv0 + cc + 1)
                    P.so.inc(sy.dma_start(xout[cc * 128:(cc + 1) * 128, :],
                                          B[:, 4096 + (cc % 2) * 2048:4096 + (cc % 2) * 2048 + 2048]), 16)
            sy.wait_ge(P.so.h, P.so.n)

        @blk.vector
        def _(v):
            for c in range(16):
                v.wait_ge(P.ld2.h, ld20 + 16 * (c + 1))
                if c >= 2:
                    v.wait_ge(P.so.h, so0 + 16 * (c - 1))
                P.dv.inc(v.tensor_tensor(
                    B[:, 4096 + (c % 2) * 2048:4096 + (c % 2) * 2048 + 2048],
                    B[:, (c % 2) * 2048:(c % 2) * 2048 + 2048],
                    X[:, c * TC:(c + 1) * TC], ALU.add))


def run_l0ffn(x_full, inputs):
    with contextlib.ExitStack() as st:
        nc = bass.Bass("TRN2", target_bir_lowering=False)
        xin = nc.dram_tensor("xin", [D, TC], F32, kind="ExternalInput")
        xout = nc.dram_tensor("xout", [D, TC], F32, kind="ExternalOutput")
        io = decl_l0ffn_io(nc)
        P = Prog(nc, st)
        P.emit_consts()
        build_l0ffn(P, io, xin, xout)
        w = prep_l0ffn_inputs(inputs)
        shards = _shard_x(x_full)
        in_maps = [dict(w, xin=shards[c]) for c in range(NCORE)]
        res = run_bass_kernel_spmd(nc, in_maps, core_ids=list(range(NCORE)))
        return _unshard_x([res.results[c]["xout"] for c in range(NCORE)])


SCALE_ATT = 1.0 / float(np.sqrt(2048.0))


def decl_l0attn_io(nc):
    io = {}
    io["l0a_xc"] = nc.dram_tensor("l0a_xc", [D, SEQ], F32, kind="ExternalInput")
    io["l0a_qkv"] = nc.dram_tensor("l0a_qkv", [12, 128, 8192], F32, kind="ExternalInput")
    io["l0a_op"] = nc.dram_tensor("l0a_op", [2, 16, 128, 1024], F32, kind="ExternalInput")
    io["l0a_gb"] = nc.dram_tensor("l0a_gb", [128, 32], F32, kind="ExternalInput")
    io["l0a_mask"] = nc.dram_tensor("l0a_mask", [TC, SEQ], BF16, kind="ExternalInput")
    io["l0a_q"] = nc.dram_tensor("l0a_q", [D, SEQ], BF16)
    io["l0a_kt"] = nc.dram_tensor("l0a_kt", [D, SEQ], BF16)
    io["l0a_v"] = nc.dram_tensor("l0a_v", [SEQ, D], BF16)
    io["l0a_y"] = nc.dram_tensor("l0a_y", [D, SEQ], BF16)
    io["l0a_ao"] = nc.dram_tensor("l0a_ao", [D, TC], BF16)
    return io


def prep_l0attn_inputs(inputs, core):
    s, h = core // 2, core % 2
    x = np.asarray(inputs["L0_attn_x"], np.float32)
    qkv = np.asarray(inputs["L0_attn_qkv"], np.float32)
    op = np.asarray(inputs["L0_attn_o_proj"], np.float32)
    out = {
        "l0a_xc": np.ascontiguousarray(x[s].T),
        "l0a_qkv": np.ascontiguousarray(
            qkv.reshape(16, 128, 12, 512).transpose(2, 1, 0, 3).reshape(12, 128, 8192)),
        "l0a_op": np.ascontiguousarray(
            op.reshape(16, 128, 2, 8, 128).transpose(2, 0, 1, 3, 4).reshape(2, 16, 128, 1024)),
        "l0a_gb": np.ascontiguousarray(np.stack(
            [np.asarray(inputs["L0_attn_gamma"], np.float32).reshape(16, 128),
             np.asarray(inputs["L0_attn_beta"], np.float32).reshape(16, 128)],
            axis=0).transpose(2, 0, 1).reshape(128, 32)),
        "l0a_mask": _l0a_mask(h),
    }
    return out


_MASK_CACHE = {}


def _l0a_mask(h):
    if h not in _MASK_CACHE:
        q = np.arange(TC)[:, None] + h * TC
        kt = np.arange(SEQ)[None, :]
        _MASK_CACHE[h] = np.where(kt <= q, 0.0, -1e30).astype(BF)
    return _MASK_CACHE[h]


def _nj(i):
    return i // 4 + 5


def build_l0attn(P, io, xout, dbg=None):
    _l0a_phaseA(P, io)
    _l0a_phaseB(P, io)
    _l0a_phaseC(P, io, xout)


def _l0a_phaseA(P, io):
    nc = P.nc
    A, B, C, C2, X = P.A, P.B, P.C, P.C2, P.X
    GBC = P.sb("l0a_GBC", [128, 32], F32)
    gbfn = lambda c: (GBC[:, c:c + 1], GBC[:, 16 + c:16 + c + 1])

    for half in range(2):
        with nc.Block(f"l0a_ld{half}") as blk:
            @blk.gpsimd
            def _(g, half=half):
                for c in range(NCH):
                    P.ld.inc(g.dma_start(
                        X[:, c * TC:(c + 1) * TC],
                        io["l0a_xc"][c * 128:(c + 1) * 128,
                                     half * TC:(half + 1) * TC]), 16)
                g.wait_ge(P.ld.h, P.ld.n)

            if half == 0:
                @blk.sync
                def _(sy):
                    P.ld2.inc(sy.dma_start(GBC[:, :], io["l0a_gb"][:, :]), 16)
                    sy.wait_ge(P.ld2.h, P.ld2.n)

        srcX = lambda c, tt: X[:, c * TC + tt * 512:c * TC + tt * 512 + 512]
        dstA = lambda c, tt: A[:, c * TC + tt * 512:c * TC + tt * 512 + 512]
        emit_ln(P, srcX, dstA, tag=f"l0a_ln1_{half}")

        with nc.Block(f"l0a_ys{half}") as blk:
            @blk.gpsimd
            def _(g, half=half):
                for c in range(NCH):
                    P.so.inc(g.dma_start(
                        io["l0a_y"][c * 128:(c + 1) * 128, half * TC:(half + 1) * TC],
                        A[:, c * TC:(c + 1) * TC]), 16)
                g.wait_ge(P.so.h, P.so.n)

        emit_ln(P, dstA, dstA, gb=gbfn, tag=f"l0a_ln2_{half}")

        with nc.Block(f"l0a_kqv{half}") as blk:
            ld0, pe0, dv0, so0 = P.ld.n, P.pe.n, P.dv.n, P.so.n

            def spill_dst(g, u, half=half):
                if g < 4:
                    m, tt = u // 4, u % 4
                    return io["l0a_q"][(g * 4 + m) * 128:(g * 4 + m + 1) * 128,
                                      half * TC + tt * 512:half * TC + tt * 512 + 512]
                if g < 8:
                    m, tt = u // 4, u % 4
                    return io["l0a_kt"][((g - 4) * 4 + m) * 128:((g - 4) * 4 + m + 1) * 128,
                                        half * TC + tt * 512:half * TC + tt * 512 + 512]
                tc = u
                return io["l0a_v"][half * TC + tc * 128:half * TC + (tc + 1) * 128,
                                   (g - 8) * 512:(g - 8) * 512 + 512]

            @blk.gpsimd
            def _(g_, half=half, ld0=ld0, pe0=pe0, dv0=dv0, so0=so0):
                for g in range(14):
                    if 2 <= g:
                        gs = g - 2
                        for u in range(16):
                            g_.wait_ge(P.dv.h, dv0 + gs * 16 + u + 1)
                            src = C2[:, ((gs * 16 + u) % 4) * 512:((gs * 16 + u) % 4) * 512 + 512]
                            P.so.inc(g_.dma_start(spill_dst(gs, u), src), 16)
                    if g < 12:
                        if g >= 2:
                            g_.wait_ge(P.pe.h, pe0 + 16 * (g - 1))
                        P.ld.inc(g_.dma_start(
                            X[:, (g % 2) * 8192:(g % 2) * 8192 + 8192],
                            io["l0a_qkv"][g]), 16)
                g_.wait_ge(P.so.h, P.so.n)

            @blk.tensor
            def _(t, ld0=ld0, pe0=pe0, dv0=dv0):
                for g in range(12):
                    t.wait_ge(P.ld.h, ld0 + 16 * (g + 1))
                    for u in range(16):
                        gu = g * 16 + u
                        if gu >= 2:
                            t.wait_ge(P.dv.h, dv0 + gu - 1)
                        ps = P.PSA[:, (gu % 2) * 512:(gu % 2) * 512 + 512]
                        if g < 8:
                            m, tt = u // 4, u % 4
                            for k in range(NCH):
                                mm = t.matmul(
                                    ps,
                                    X[:, (g % 2) * 8192 + k * 512 + m * 128:
                                      (g % 2) * 8192 + k * 512 + m * 128 + 128],
                                    A[:, k * TC + tt * 512:k * TC + tt * 512 + 512],
                                    start=(k == 0), stop=(k == NCH - 1))
                        else:
                            tc = u
                            for k in range(NCH):
                                mm = t.matmul(
                                    ps,
                                    A[:, k * TC + tc * 128:k * TC + tc * 128 + 128],
                                    X[:, (g % 2) * 8192 + k * 512:(g % 2) * 8192 + k * 512 + 512],
                                    start=(k == 0), stop=(k == NCH - 1))
                        P.pe.inc(mm)

            @blk.vector
            def _(v, pe0=pe0, dv0=dv0, so0=so0):
                for g in range(12):
                    for u in range(16):
                        gu = g * 16 + u
                        v.wait_ge(P.pe.h, pe0 + gu + 1)
                        if gu >= 4:
                            v.wait_ge(P.so.h, so0 + 16 * (gu - 3))
                        ps = P.PSA[:, (gu % 2) * 512:(gu % 2) * 512 + 512]
                        P.dv.inc(v.tensor_copy(
                            C2[:, (gu % 4) * 512:(gu % 4) * 512 + 512], ps))


def _l0a_phaseB(P, io):
    nc = P.nc
    A, B, C, C2, X = P.A, P.B, P.C, P.C2, P.X
    with nc.Block("l0a_qpre") as blk:
        @blk.gpsimd
        def _(g):
            P.ld.inc(g.dma_start(
                A[:, :].rearrange("p (c n) -> p c n", c=16),
                bass.AP(io["l0a_q"], P.dynoff(g, 0),
                        [[SEQ, 128], [128 * SEQ, 16], [1, TC]]),
                bounds_check=None), 16)
            g.wait_ge(P.ld.h, P.ld.n)

    for i in range(16):
        nj = _nj(i)
        with nc.Block(f"l0a_at{i}") as blk:
            ld0, ld20, pe0, dv0, ac0, so0 = P.ld.n, P.ld2.n, P.pe.n, P.dv.n, P.ac.n, P.so.n

            @blk.gpsimd
            def _(g, i=i, nj=nj, dv0=dv0, pe0=pe0):
                for j in range(nj):
                    if j >= 2:
                        g.wait_ge(P.pe.h, pe0 + j - 1)
                        g.wait_ge(P.dv.h, dv0 + j - 1)
                    P.ld.inc(g.dma_start(
                        X[:, (j % 2) * 8192:(j % 2) * 8192 + 8192].rearrange(
                            "p (c n) -> p c n", c=16),
                        io["l0a_kt"].rearrange("(c p) t -> p c t", p=128)[
                            :, :, j * 512:(j + 1) * 512]), 16)
                    P.ld.inc(g.dma_start(
                        C[:, (j % 2) * 512:(j % 2) * 512 + 512],
                        io["l0a_mask"][i * 128:(i + 1) * 128, j * 512:(j + 1) * 512]), 16)
                for j in range(nj):
                    if j >= 2:
                        g.wait_ge(P.pe.h, pe0 + nj + 1 + 2 * (j - 1))
                    P.ld.inc(g.dma_start(
                        X[:, 16384 + (j % 2) * 8192:16384 + (j % 2) * 8192 + 8192].rearrange(
                            "p (c n) -> p c n", c=4),
                        io["l0a_v"].rearrange("(q p) d -> p q d", p=128)[
                            :, j * 4:(j + 1) * 4, :]), 16)
                g.wait_ge(P.so.h, P.so.n)

            @blk.tensor
            def _(t, i=i, nj=nj, ld0=ld0, dv0=dv0, ac0=ac0):
                for j in range(nj):
                    t.wait_ge(P.ld.h, ld0 + 32 * (j + 1))
                    if j >= 2:
                        t.wait_ge(P.dv.h, dv0 + j - 1)
                    ps = P.PSA[:, (j % 2) * 512:(j % 2) * 512 + 512]
                    for c in range(NCH):
                        mm = t.matmul(ps,
                                      A[:, c * TC + i * 128:c * TC + i * 128 + 128],
                                      X[:, (j % 2) * 8192 + c * 512:(j % 2) * 8192 + c * 512 + 512],
                                      start=(c == 0), stop=(c == NCH - 1))
                    P.pe.inc(mm)
                for j in range(nj):
                    t.wait_ge(P.ac.h, ac0 + 1)
                    t.wait_ge(P.ld.h, ld0 + 32 * nj + 16 * (j + 1))
                    if j >= 1:
                        t.wait_ge(P.dv.h, dv0 + nj + 2 + j)
                    for kc in range(4):
                        t.transpose(P.PSA[:, 1024 + kc * 128:1024 + kc * 128 + 128],
                                    B[:, j * 512 + kc * 128:j * 512 + kc * 128 + 128],
                                    P.ident[:, :])
                    P.pe.inc(t.engine_nop() if False else t.transpose(
                        P.PSA[:, 1024:1152], B[:, j * 512:j * 512 + 128], P.ident[:, :]))
                    t.wait_ge(P.dv.h, dv0 + nj + 3 + j)
                    for dvt in range(4):
                        for kc in range(4):
                            mm = t.matmul(
                                P.PSB[:, dvt * 512:dvt * 512 + 512],
                                C2[:, 2048 + kc * 128:2048 + kc * 128 + 128],
                                X[:, 16384 + (j % 2) * 8192 + kc * 2048 + dvt * 512:
                                  16384 + (j % 2) * 8192 + kc * 2048 + dvt * 512 + 512],
                                start=(j == 0 and kc == 0), stop=(j == nj - 1 and kc == 3))
                    P.pe.inc(mm)
                t.wait_ge(P.dv.h, dv0 + nj + 2 + nj + 4)
                for m in range(NCH):
                    mm = t.transpose(P.PSA[:, 1024 + (m % 4) * 128:1024 + (m % 4) * 128 + 128],
                                     B[:, m * 128:(m + 1) * 128], P.ident[:, :])
                    if m % 4 == 3:
                        P.pe.inc(mm)
                        t.wait_ge(P.dv.h, dv0 + nj + 2 + nj + 4 + m // 4 + 1)

            @blk.vector
            def _(v, i=i, nj=nj, ld0=ld0, pe0=pe0, ac0=ac0, so0=so0):
                for j in range(nj):
                    v.wait_ge(P.pe.h, pe0 + j + 1)
                    P.dv.inc(v.scalar_tensor_tensor(
                        B[:, j * 512:j * 512 + 512],
                        P.PSA[:, (j % 2) * 512:(j % 2) * 512 + 512],
                        SCALE_ATT, C[:, (j % 2) * 512:(j % 2) * 512 + 512],
                        ALU.mult, ALU.add))
                P.dv.inc(v.reduce_max(B[:, 8704:8705], B[:, 0:nj * 512],
                                      axis=AX, negate=True))
                v.wait_ge(P.ac.h, ac0 + 1)
                P.dv.inc(v.reciprocal(B[:, 8706:8707], B[:, 8705:8706]))
                for j in range(nj):
                    v.wait_ge(P.pe.h, pe0 + nj + 2 * j + 1)
                    P.dv.inc(v.tensor_copy(C2[:, 2048:2560], P.PSA[:, 1024:1536]))
                v.wait_ge(P.pe.h, pe0 + 3 * nj)
                for dvt in range(4):
                    P.dv.inc(v.tensor_scalar_mul(
                        B[:, dvt * 512:dvt * 512 + 512],
                        P.PSB[:, dvt * 512:dvt * 512 + 512], B[:, 8706:8707]))
                for mq in range(4):
                    v.wait_ge(P.pe.h, pe0 + 3 * nj + mq + 1)
                    P.dv.inc(v.tensor_copy(
                        C2[:, mq * 512:mq * 512 + 512], P.PSA[:, 1024:1536]))

            @blk.scalar
            def _(s, i=i, nj=nj, dv0=dv0):
                s.wait_ge(P.dv.h, dv0 + nj + 1)
                P.ac.inc(s.activation(B[:, 0:nj * 512], B[:, 0:nj * 512], AF.Exp,
                                      bias=B[:, 8704:8705], scale=1.0,
                                      accum_out=B[:, 8705:8706]))

            @blk.sync
            def _(sy, i=i, nj=nj):
                sy.wait_ge(P.dv.h, P.dv.n)
                dst = io["l0a_ao"].rearrange("(m p) t -> p m t", p=128)[
                    :, :, i * 128:(i + 1) * 128]
                P.so.inc(sy.dma_start(
                    dst, C2[:, 0:2048].rearrange("p (m n) -> p m n", m=16)), 16)
                sy.wait_ge(P.so.h, P.so.n)


def _l0a_phaseC(P, io, xout):
    nc = P.nc
    A, B, C, C2, X = P.A, P.B, P.C, P.C2, P.X
    for tt in range(4):
        with nc.Block(f"l0a_op{tt}") as blk:
            ld0, pe0, dv0, so0 = P.ld.n, P.pe.n, P.dv.n, P.so.n

            @blk.gpsimd
            def _(g, tt=tt, pe0=pe0, dv0=dv0):
                P.ld.inc(g.dma_start(
                    C[:, 0:8192].rearrange("p (c n) -> p c n", c=16),
                    io["l0a_ao"].rearrange("(c p) t -> p c t", p=128)[
                        :, :, tt * 512:(tt + 1) * 512]), 16)
                for mh in range(2):
                    if mh == 1:
                        g.wait_ge(P.dv.h, dv0 + 16)
                    cst = tt * 512 + mh * 8 * 128 * SEQ
                    P.ld.inc(g.dma_start(
                        B[:, 4096:8192].rearrange("p (c n) -> p c n", c=8),
                        bass.AP(io["l0a_xc"], P.dynoff(g, cst),
                                [[SEQ, 128], [128 * SEQ, 8], [1, 512]]),
                        bounds_check=None), 16)
                    P.ld.inc(g.dma_start(
                        A[:, 0:4096].rearrange("p (c n) -> p c n", c=8),
                        bass.AP(io["l0a_y"], P.dynoff(g, cst),
                                [[SEQ, 128], [128 * SEQ, 8], [1, 512]]),
                        bounds_check=None), 16)
                    for k in range(NCH):
                        kk = mh * NCH + k
                        if kk >= 2:
                            g.wait_ge(P.pe.h, pe0 + (kk - 1))
                        P.ld.inc(g.dma_start(
                            C2[:, (kk % 2) * 1024:(kk % 2) * 1024 + 1024],
                            io["l0a_op"][mh, k]), 16)
                g.wait_ge(P.so.h, P.so.n)

            @blk.tensor
            def _(t, tt=tt, ld0=ld0, dv0=dv0):
                t.wait_ge(P.ld.h, ld0 + 16)
                for mh in range(2):
                    if mh == 1:
                        t.wait_ge(P.dv.h, dv0 + 16)
                    for k in range(NCH):
                        kk = mh * NCH + k
                        t.wait_ge(P.ld.h, ld0 + 16 + 32 * (mh + 1) + 16 * (kk + 1))
                        for m in range(8):
                            ps = (P.PSA if m < 4 else P.PSB)
                            psl = ps[:, (m % 4) * 512:(m % 4) * 512 + 512]
                            mm = t.matmul(psl,
                                          C2[:, (kk % 2) * 1024 + m * 128:
                                             (kk % 2) * 1024 + m * 128 + 128],
                                          C[:, k * 512:(k + 1) * 512],
                                          start=(k == 0), stop=(k == NCH - 1))
                        P.pe.inc(mm)

            @blk.vector
            def _(v, tt=tt, ld0=ld0, pe0=pe0, so0=so0):
                for mh in range(2):
                    v.wait_ge(P.pe.h, pe0 + NCH * (mh + 1))
                    v.wait_ge(P.ld.h, ld0 + 16 + 32 * (mh + 1) + 16 * NCH * (mh + 1))
                    if mh == 1:
                        v.wait_ge(P.so.h, so0 + 16 * 8)
                    for m in range(8):
                        ps = (P.PSA if m < 4 else P.PSB)
                        psl = ps[:, (m % 4) * 512:(m % 4) * 512 + 512]
                        out = B[:, m * 512:m * 512 + 512]
                        v.tensor_tensor(out, psl, B[:, 4096 + m * 512:4096 + m * 512 + 512],
                                        ALU.add)
                        P.dv.inc(v.tensor_tensor(
                            out, out, A[:, m * 512:m * 512 + 512], ALU.add), 2)

            @blk.sync
            def _(sy, tt=tt, dv0=dv0):
                for mh in range(2):
                    for m in range(8):
                        sy.wait_ge(P.dv.h, dv0 + 16 * mh + 2 * (m + 1))
                        c = mh * 8 + m
                        P.so.inc(sy.dma_start(
                            xout[c * 128:(c + 1) * 128, tt * 512:(tt + 1) * 512],
                            B[:, m * 512:m * 512 + 512]), 16)
                sy.wait_ge(P.so.h, P.so.n)


def run_l0attn(x_full, inputs):
    with contextlib.ExitStack() as st:
        nc = bass.Bass("TRN2", target_bir_lowering=False)
        xout = nc.dram_tensor("xout", [D, TC], F32, kind="ExternalOutput")
        io = decl_l0attn_io(nc)
        P = Prog(nc, st)
        P.emit_consts()
        build_l0attn(P, io, xout)
        shared = prep_l0attn_inputs(inputs, 0)
        in_maps = []
        for c in range(NCORE):
            m = prep_l0attn_inputs(inputs, c)
            m["l0a_qkv"] = shared["l0a_qkv"]
            m["l0a_op"] = shared["l0a_op"]
            m["l0a_gb"] = shared["l0a_gb"]
            in_maps.append(m)
        res = run_bass_kernel_spmd(nc, in_maps, core_ids=list(range(NCORE)))
        return _unshard_x([res.results[c]["xout"] for c in range(NCORE)])


XSLOT = 81920
XO_HALO = 0
XO_BD = 8192
XO_BO = 8192 + 32768
XO_FL1 = 73728
XO_FL2 = 73732


def decl_l1attn_io(nc):
    io = {}
    io["l1a_ns"] = nc.dram_tensor("l1a_ns", [128, 16], F32, kind="ExternalInput")
    io["l1a_cw"] = nc.dram_tensor("l1a_cw", [128, 64], F32, kind="ExternalInput")
    io["l1a_cb"] = nc.dram_tensor("l1a_cb", [128, 16], F32, kind="ExternalInput")
    io["l1a_dtb"] = nc.dram_tensor("l1a_dtb", [128, 16], F32, kind="ExternalInput")
    io["l1a_ipw"] = nc.dram_tensor("l1a_ipw", [48, 128, 2048], F32, kind="ExternalInput")
    io["l1a_cosF"] = nc.dram_tensor("l1a_cosF", [128, 2048], F32, kind="ExternalInput")
    io["l1a_sinP"] = nc.dram_tensor("l1a_sinP", [128, 2048], F32, kind="ExternalInput")
    io["l1a_hm"] = nc.dram_tensor("l1a_hm", [16, 128, 2048], F32, kind="ExternalInput")
    io["l1a_opw"] = nc.dram_tensor("l1a_opw", [16, 128, 2048], F32, kind="ExternalInput")
    io["l1a_hgate"] = nc.dram_tensor("l1a_hgate", [128, 1], F32, kind="ExternalInput")
    io["l1a_nonce"] = nc.dram_tensor("l1a_nonce", [1, 2], U32, kind="ExternalInput")
    io["l1a_dt"] = nc.dram_tensor("l1a_dt", [D, TC], F32)
    io["l1a_gt"] = nc.dram_tensor("l1a_gt", [D, TC], BF16)
    io["l1a_E"] = nc.dram_tensor("l1a_E", [D, TC], BF16)
    io["l1a_CO"] = nc.dram_tensor("l1a_CO", [D, TC], BF16)
    io["l1a_h"] = nc.dram_tensor("l1a_h", [D, TC], BF16)
    io["l1a_xch"] = nc.dram_tensor("l1a_xch", [NCORE * XSLOT], F32, addr_space="Shared")
    io["l1a_car"] = nc.dram_tensor("l1a_car", [128, 256], F32)
    return io


def prep_l1attn_inputs(inputs, core, nonce):
    h = core % 2
    def cols(v, n):
        return np.ascontiguousarray(np.asarray(v, np.float32).reshape(n, 128).T)
    ipw = np.asarray(inputs["L1_attn_in_proj_w"], np.float32).T
    hmix = np.asarray(inputs["L1_attn_head_mix"], np.float32)
    mbig = np.zeros((2048, 2048), np.float32)
    for hh in range(16):
        for m in range(16):
            if hmix[hh, m] != 0.0:
                idx = np.arange(128)
                mbig[hh * 128 + idx, m * 128 + idx] = hmix[hh, m]
    opw = np.asarray(inputs["L1_attn_out_proj_w"], np.float32).T
    cw = np.asarray(inputs["L1_attn_conv_w"], np.float32).reshape(2048, 4)

    t = np.arange(h * TC, (h + 1) * TC, dtype=np.float64)
    i = np.arange(64, dtype=np.float64)
    invf = 1.0 / (10000.0 ** (i / 64.0))
    fr = t[None, :] * invf[:, None]
    cosv = np.cos(fr).astype(np.float32)
    sinv = np.sin(fr).astype(np.float32)
    cosF = np.concatenate([cosv, cosv], 0)
    sinP = np.concatenate([-sinv, sinv], 0)

    def coltile(w, m):
        return np.ascontiguousarray(
            w.reshape(16, 128, m, 128).transpose(2, 1, 0, 3).reshape(m, 128, 2048))

    return {
        "l1a_ns": cols(inputs["L1_attn_norm_scale"], 16),
        "l1a_cw": np.ascontiguousarray(
            cw.reshape(16, 128, 4).transpose(1, 0, 2).reshape(128, 64)),
        "l1a_cb": cols(inputs["L1_attn_conv_b"], 16),
        "l1a_dtb": cols(inputs["L1_attn_dt_bias"], 16),
        "l1a_ipw": coltile(ipw, 48),
        "l1a_cosF": cosF,
        "l1a_sinP": sinP,
        "l1a_hm": coltile(mbig, 16),
        "l1a_opw": coltile(opw, 16),
        "l1a_hgate": np.full((128, 1), float(h), np.float32),
        "l1a_nonce": nonce,
    }


def emit_rms(P, src, dst, nscol, eps=1e-6, tag="rms"):
    nc = P.nc
    for rnd in range(2):
        tts = [2 * rnd, 2 * rnd + 1]
        with nc.Block(f"{tag}_s{rnd}") as blk:
            ac0, pe0 = P.ac.n, P.pe.n

            @blk.scalar
            def _(s, tts=tts, pe0=pe0):
                for i, tt in enumerate(tts):
                    for c in range(NCH):
                        k = i * NCH + c
                        sl = P.C[:, (k % 8) * 512:(k % 8) * 512 + 512]
                        if k >= 8:
                            s.wait_ge(P.pe.h, pe0 + (k - 8 + 1))
                        P.ac.inc(s.activation(sl, src(c, tt), AF.Square))

            @blk.tensor
            def _(t, tts=tts, ac0=ac0):
                for i, tt in enumerate(tts):
                    ps_q = P.PSA[0:1, i * 512:i * 512 + 512]
                    for c in range(NCH):
                        k = i * NCH + c
                        sl = P.C[:, (k % 8) * 512:(k % 8) * 512 + 512]
                        t.wait_ge(P.ac.h, ac0 + k + 1)
                        P.pe.inc(t.matmul(ps_q, P.ones_col[:, :], sl,
                                          start=(c == 0), stop=(c == NCH - 1)))

            @blk.vector
            def _(v):
                v.wait_ge(P.pe.h, P.pe.n)
                q_sp = P.R[0:1, 0:1024]
                t_sp = P.R[0:1, 2048:3072]
                for i in range(2):
                    v.tensor_copy(P.R[0:1, i * 512:i * 512 + 512],
                                  P.PSA[0:1, i * 512:i * 512 + 512])
                v.tensor_scalar_mul(q_sp, q_sp, 1.0 / D)
                v.tensor_scalar_add(q_sp, q_sp, eps)
                P.dv.inc(v.reciprocal(t_sp, q_sp))

            @blk.scalar
            def _(s):
                s.wait_ge(P.dv.h, P.dv.n)
                P.ac.inc(s.activation(P.R[0:1, 0:1024],
                                      P.R[0:1, 2048:3072], AF.Sqrt))

        with nc.Block(f"{tag}_a{rnd}") as blk:
            @blk.tensor
            def _(t):
                for i in range(2):
                    a_row = P.R[0:1, i * 512:i * 512 + 512]
                    P.pe.inc(t.matmul(P.PSB[:, i * 1024:i * 1024 + 512],
                                      P.ones_row[:, :], a_row, start=True, stop=True))

            @blk.vector
            def _(v, tts=tts):
                v.wait_ge(P.pe.h, P.pe.n)
                for i, tt in enumerate(tts):
                    bcA = P.PSB[:, i * 1024:i * 1024 + 512]
                    for c in range(NCH):
                        tmp = P.B[:, 8192 + (c % 2) * 512:8704 + (c % 2) * 512]
                        v.tensor_tensor(tmp, src(c, tt), bcA, ALU.mult)
                        v.tensor_scalar(tmp, tmp, nscol(c), 60000.0, ALU.mult, ALU.min)
                        P.dv.inc(v.tensor_scalar_max(dst(c, tt), tmp, -60000.0))


def build_l1attn(P, io, xin, xout):
    nc = P.nc
    A, B, C, C2, X = P.A, P.B, P.C, P.C2, P.X
    NSC = P.sb("l1a_NSC", [128, 16], F32)
    CWC = P.sb("l1a_CWC", [128, 64], F32)
    CBC = P.sb("l1a_CBC", [128, 16], F32)
    DTB = P.sb("l1a_DTB", [128, 16], F32)
    HGT = P.sb("l1a_HGT", [128, 1], F32)
    NON = P.sb("l1a_NON", [1, 2], U32)
    FLG = P.sb("l1a_FLG", [1, 2], U32)
    XHS = P.sb("l1a_XH", [128, 48], BF16)
    BDS = P.sb("l1a_BD", [128, 128], F32)
    BOS = P.sb("l1a_BO", [128, 128], F32)
    CST = P.sb("l1a_CST", [128, 528], F32)

    with nc.Block("t1_ld") as blk:
        @blk.gpsimd
        def _(g):
            for c in range(NCH):
                P.ld.inc(g.dma_start(X[:, c * TC:(c + 1) * TC],
                                     xin[c * 128:(c + 1) * 128, :]), 16)
            g.reg_mul(P.rg_b, P.rg_pid, XSLOT)
            g.reg_alu(P.rg_c, P.rg_pid, 1, ALU.bitwise_xor)
            g.reg_mul(P.rg_c, P.rg_c, XSLOT)
            g.reg_alu(P.rg_d, P.rg_own, 11, ALU.arith_shift_right)
            g.reg_mul(P.rg_d, P.rg_d, 8)
            g.reg_alu(P.rg_e, P.rg_pid, 6, ALU.bitwise_and)
            g.reg_mul(P.rg_e, P.rg_e, XSLOT)

        @blk.sync
        def _(sy):
            for name, t_ in (("l1a_ns", NSC), ("l1a_cw", CWC), ("l1a_cb", CBC),
                             ("l1a_dtb", DTB), ("l1a_hgate", HGT), ("l1a_nonce", NON)):
                P.ld2.inc(sy.dma_start(t_[:, :], io[name][:, :]), 16)
            sy.wait_ge(P.ld2.h, P.ld2.n)

        @blk.vector
        def _(v):
            v.memset(CST[:, 0:256], -20.0)
            v.memset(CST[:, 256:512], 0.0)
            P.dv.inc(v.memset(CST[:, 512:528], -80.0))

    srcX = lambda c, tt: X[:, c * TC + tt * 512:c * TC + tt * 512 + 512]
    dstA = lambda c, tt: A[:, c * TC + tt * 512:c * TC + tt * 512 + 512]
    emit_ln(P, srcX, dstA, tag="t1_ln")

    with nc.Block("t1_hs") as blk:
        @blk.gpsimd
        def _(g):
            for c in range(NCH):
                P.so.inc(g.dma_start(io["l1a_h"][c * 128:(c + 1) * 128, :],
                                     A[:, c * TC:(c + 1) * TC]), 16)
            g.wait_ge(P.so.h, P.so.n)

    emit_rms(P, dstA, srcX, lambda c: NSC[:, c:c + 1], tag="t1_rm")

    with nc.Block("t1_hx") as blk:
        @blk.gpsimd
        def _(g):
            g.reg_add(P.rg_a, P.rg_b, XO_HALO)
            P.so.inc(g.dma_start(
                bass.AP(io["l1a_xch"], P.rg_a, [[48, 128], [3, 16], [1, 3]]),
                X[:, :].rearrange("p (c n) -> p c n", c=16)[:, :, 2045:2048],
                bounds_check=None), 16)
            g.wait_ge(P.so.h, P.so.n)
            g.reg_add(P.rg_a, P.rg_b, XO_FL1)
            P.so.inc(g.dma_start(
                bass.AP(io["l1a_xch"], P.rg_a, [[2, 1], [1, 2]]),
                NON[:, :], bounds_check=None), 16)
            g.wait_ge(P.so.h, P.so.n)
            g.reg_add(P.rg_a, P.rg_c, XO_FL1)
            with g.register("t1w") as wt, g.register("t1c") as cond, \
                    g.register("t1l") as flo, g.register("t1h") as fhi, \
                    g.register("t1nl") as nlo, g.register("t1nh") as nhi:
                g.reg_load(nlo, NON[0:1, 0:1])
                g.reg_load(nhi, NON[0:1, 1:2])
                g.reg_mov(wt, P.ld.n)
                g.reg_mov(cond, 1)
                with g.While(cond):
                    g.dma_start(FLG[:, :],
                                bass.AP(io["l1a_xch"], P.rg_a, [[2, 1], [1, 2]]),
                                bounds_check=None).then_inc(P.ld.h, 16)
                    g.reg_add(wt, wt, 16)
                    g.wait_ge(P.ld.h, wt)
                    g.reg_load(flo, FLG[0:1, 0:1])
                    g.reg_load(fhi, FLG[0:1, 1:2])
                    g.reg_alu(flo, flo, nlo, ALU.not_equal)
                    g.reg_alu(fhi, fhi, nhi, ALU.not_equal)
                    g.reg_alu(cond, flo, fhi, ALU.bitwise_or)
            P.ld.n = None
            g.reg_add(P.rg_a, P.rg_c, XO_HALO)
            g.dma_start(XHS[:, :],
                        bass.AP(io["l1a_xch"], P.rg_a, [[48, 128], [1, 48]]),
                        bounds_check=None).then_inc(P.po.h, 16)
            P.po.n += 16
            g.wait_ge(P.po.h, P.po.n)

        @blk.vector
        def _(v):
            v.wait_ge(P.po.h, P.po.n)
            P.dv.inc(v.tensor_scalar_mul(XHS[:, :], XHS[:, :], HGT[:, 0:1]))

    with nc.Block("t1_ldsync") as blk:
        @blk.gpsimd
        def _(g):
            pass
    P.ld = Sem(P.st, nc, "s_ld_b")

    with nc.Block("t1_cv") as blk:
        dv0, ac0 = P.dv.n, P.ac.n

        @blk.vector
        def _(v):
            for c in range(NCH):
                b0 = B[:, (c % 2) * 2048:(c % 2) * 2048 + 2048]
                if c >= 2:
                    v.wait_ge(P.ac.h, ac0 + (c - 2) + 1)
                xc_ = X[:, c * TC:(c + 1) * TC]
                v.tensor_scalar_mul(b0, xc_, CWC[:, c * 4 + 3:c * 4 + 4])
                for k in (2, 1, 0):
                    sh = 3 - k
                    v.scalar_tensor_tensor(b0[:, sh:2048], xc_[:, 0:2048 - sh],
                                           CWC[:, c * 4 + k:c * 4 + k + 1],
                                           b0[:, sh:2048], ALU.mult, ALU.add)
                    ins = v.scalar_tensor_tensor(b0[:, 0:sh], XHS[:, c * 3 + k:c * 3 + 3],
                                                 CWC[:, c * 4 + k:c * 4 + k + 1],
                                                 b0[:, 0:sh], ALU.mult, ALU.add)
                P.dv.inc(ins)

        @blk.scalar
        def _(s):
            for c in range(NCH):
                s.wait_ge(P.dv.h, dv0 + c + 1)
                P.ac.inc(s.activation(A[:, c * TC:(c + 1) * TC],
                                      B[:, (c % 2) * 2048:(c % 2) * 2048 + 2048],
                                      AF.Silu, bias=CBC[:, c:c + 1]))

    def ip_mms(t, m, slot, ld_tgt, psum_of):
        t.wait_ge(P.ld.h, ld_tgt)
        ps = psum_of(m)
        for tt in range(4):
            for k in range(NCH):
                mm = t.matmul(ps[:, tt * 512:tt * 512 + 512],
                              C[:, slot * 2048 + k * 128:slot * 2048 + k * 128 + 128],
                              A[:, k * TC + tt * 512:k * TC + tt * 512 + 512],
                              start=(k == 0), stop=(k == NCH - 1))
            P.pe.inc(mm)

    with nc.Block("t1_ipdt") as blk:
        ld0, pe0, dv0, ac0, so0 = P.ld.n, P.pe.n, P.dv.n, P.ac.n, P.so.n

        @blk.gpsimd
        def _(g):
            for m in range(18):
                if m < 16:
                    if m >= 4:
                        g.wait_ge(P.pe.h, pe0 + 4 * (m - 3))
                    P.ld.inc(g.dma_start(C[:, (m % 4) * 2048:(m % 4) * 2048 + 2048],
                                         io["l1a_ipw"][m]), 16)
                if m >= 2:
                    ms = m - 2
                    for tt in range(4):
                        u = 4 * ms + tt
                        g.wait_ge(P.dv.h, dv0 + 2 * u + 2)
                        P.so.inc(g.dma_start(
                            io["l1a_dt"][ms * 128:(ms + 1) * 128, tt * 512:(tt + 1) * 512],
                            B[:, 3072 + (u % 2) * 512:3072 + (u % 2) * 512 + 512]), 16)

        @blk.tensor
        def _(t):
            for m in range(16):
                if m >= 2:
                    t.wait_ge(P.dv.h, dv0 + 8 * (m - 1))
                ip_mms(t, m, m % 4, ld0 + 16 * (m + 1),
                       lambda m: P.PSA if m % 2 == 0 else P.PSB)

        @blk.vector
        def _(v):
            for m in range(16):
                ps = P.PSA if m % 2 == 0 else P.PSB
                for tt in range(4):
                    u = 4 * m + tt
                    v.wait_ge(P.pe.h, pe0 + u + 1)
                    bz = B[:, (u % 2) * 512:(u % 2) * 512 + 512]
                    br = B[:, 1024 + (u % 2) * 512:1024 + (u % 2) * 512 + 512]
                    bd = B[:, 3072 + (u % 2) * 512:3072 + (u % 2) * 512 + 512]
                    ba = B[:, 2048 + (u % 2) * 512:2048 + (u % 2) * 512 + 512]
                    P.dv.inc(v.tensor_scalar_add(bz, ps[:, tt * 512:tt * 512 + 512],
                                                 DTB[:, m:m + 1]))
                    v.tensor_scalar_max(br, bz, 0.0)
                    v.wait_ge(P.ac.h, ac0 + 2 * u + 2)
                    if u >= 2:
                        v.wait_ge(P.so.h, so0 + 16 * (u - 1))
                    v.tensor_add(bd, ba, br)
                    P.dv.inc(v.tensor_scalar(bd, bd, 0.001, 2.0, ALU.max, ALU.min))

        @blk.scalar
        def _(s):
            for m in range(16):
                for tt in range(4):
                    u = 4 * m + tt
                    s.wait_ge(P.dv.h, dv0 + 2 * u + 1)
                    if u >= 2:
                        s.wait_ge(P.dv.h, dv0 + 2 * (u - 2) + 2)
                    bz = B[:, (u % 2) * 512:(u % 2) * 512 + 512]
                    ba = B[:, 2048 + (u % 2) * 512:2048 + (u % 2) * 512 + 512]
                    P.ac.inc(s.activation(ba, bz, AF.Abs))
                    s.activation(ba, ba, AF.Exp, scale=-1.0)
                    P.ac.inc(s.activation(ba, ba, AF.Ln, bias=1.0))

    with nc.Block("t1_ipv") as blk:
        ld0, pe0, dv0 = P.ld.n, P.pe.n, P.dv.n

        @blk.gpsimd
        def _(g):
            for mi in range(16):
                if mi >= 4:
                    g.wait_ge(P.pe.h, pe0 + 4 * (mi - 3))
                P.ld.inc(g.dma_start(C[:, (mi % 4) * 2048:(mi % 4) * 2048 + 2048],
                                     io["l1a_ipw"][16 + mi]), 16)

        @blk.tensor
        def _(t):
            for mi in range(16):
                if mi >= 2:
                    t.wait_ge(P.dv.h, dv0 + 4 * (mi - 1))
                ip_mms(t, mi, mi % 4, ld0 + 16 * (mi + 1),
                       lambda m: P.PSA if m % 2 == 0 else P.PSB)

        @blk.vector
        def _(v):
            for mi in range(16):
                ps = P.PSA if mi % 2 == 0 else P.PSB
                for tt in range(4):
                    v.wait_ge(P.pe.h, pe0 + 4 * mi + tt + 1)
                    P.dv.inc(v.tensor_copy(
                        X[:, mi * TC + tt * 512:mi * TC + tt * 512 + 512],
                        ps[:, tt * 512:tt * 512 + 512]))

    with nc.Block("t1_ipg") as blk:
        ld0, pe0, ac0, so0 = P.ld.n, P.pe.n, P.ac.n, P.so.n

        @blk.gpsimd
        def _(g):
            for m in range(18):
                if m < 16:
                    if m >= 4:
                        g.wait_ge(P.pe.h, pe0 + 4 * (m - 3))
                    P.ld.inc(g.dma_start(C[:, (m % 4) * 2048:(m % 4) * 2048 + 2048],
                                         io["l1a_ipw"][32 + m]), 16)
                if m >= 2:
                    ms = m - 2
                    for tt in range(4):
                        u = 4 * ms + tt
                        g.wait_ge(P.ac.h, ac0 + u + 1)
                        P.so.inc(g.dma_start(
                            io["l1a_gt"][ms * 128:(ms + 1) * 128, tt * 512:(tt + 1) * 512],
                            C2[:, (u % 4) * 512:(u % 4) * 512 + 512]), 16)
            g.wait_ge(P.so.h, P.so.n)

        @blk.tensor
        def _(t):
            for m in range(16):
                if m >= 2:
                    t.wait_ge(P.ac.h, ac0 + 4 * (m - 1))
                ip_mms(t, m, m % 4, ld0 + 16 * (m + 1),
                       lambda m: P.PSA if m % 2 == 0 else P.PSB)

        @blk.scalar
        def _(s):
            for m in range(16):
                ps = P.PSA if m % 2 == 0 else P.PSB
                for tt in range(4):
                    u = 4 * m + tt
                    s.wait_ge(P.pe.h, pe0 + u + 1)
                    if u >= 4:
                        s.wait_ge(P.so.h, so0 + 16 * (u - 3))
                    P.ac.inc(s.activation(C2[:, (u % 4) * 512:(u % 4) * 512 + 512],
                                          ps[:, tt * 512:tt * 512 + 512], AF.Sigmoid))

    rep16 = lambda base: bass.AP(P.B, base, [[9216, 128], [0, 16], [1, 256]])
    for q in range(8):
        with nc.Block(f"t1_sc{q}") as blk:
            ld0, dv0, ac0, so0, po0 = P.ld.n, P.dv.n, P.ac.n, P.so.n, P.po.n

            @blk.gpsimd
            def _(g, q=q):
                P.ld.inc(g.dma_start(
                    B[:, 0:4096].rearrange("p (c n) -> p c n", c=16),
                    io["l1a_dt"].rearrange("(c p) t -> p c t", p=128)[
                        :, :, q * 256:(q + 1) * 256]), 16)
                P.ld.inc(g.dma_start(B[:, 8192:8448],
                                     io["l1a_cosF"][:, q * 256:(q + 1) * 256]), 16)
                P.ld.inc(g.dma_start(B[:, 8448:8704],
                                     io["l1a_sinP"][:, q * 256:(q + 1) * 256]), 16)
                vsl = X[:, :].rearrange("p (c n) -> p c n", c=16)[
                    :, :, q * 256:(q + 1) * 256]
                P.ld.inc(g.dma_start(
                    C2[0:64, :].rearrange("p (c n) -> p c n", c=16),
                    X[64:128, :].rearrange("p (c n) -> p c n", c=16)[
                        :, :, q * 256:(q + 1) * 256]), 16)
                P.ld.inc(g.dma_start(
                    C2[64:128, :].rearrange("p (c n) -> p c n", c=16),
                    X[0:64, :].rearrange("p (c n) -> p c n", c=16)[
                        :, :, q * 256:(q + 1) * 256]), 16)
                g.wait_ge(P.dv.h, dv0 + 6 + 32)
                P.so.inc(g.dma_start(
                    io["l1a_E"].rearrange("(c p) t -> p c t", p=128)[
                        :, :, q * 256:(q + 1) * 256],
                    C[:, 0:4096].rearrange("p (c n) -> p c n", c=16)), 16)
                g.wait_ge(P.dv.h, dv0 + 7 + 32)
                P.so.inc(g.dma_start(
                    io["l1a_CO"].rearrange("(c p) t -> p c t", p=128)[
                        :, :, q * 256:(q + 1) * 256],
                    C2[:, 0:4096].rearrange("p (c n) -> p c n", c=16)), 16)
                g.wait_ge(P.so.h, P.so.n)

            @blk.vector
            def _(v, q=q):
                v.wait_ge(P.ld.h, P.ld.n)
                vt = X[:, :].rearrange("p (c n) -> p c n", c=16)[
                    :, :, q * 256:(q + 1) * 256]
                vr = B[:, 4096:8192].rearrange("p (c n) -> p c n", c=16)
                v.tensor_tensor(vr, vt, rep16(8192), ALU.mult)
                v.tensor_tensor(P.PSA[:, 0:2048].rearrange("p (c n) -> p c n", c=8),
                                C2[:, 0:2048].rearrange("p (c n) -> p c n", c=8),
                                bass.AP(P.B, 8448, [[9216, 128], [0, 8], [1, 256]]),
                                ALU.mult)
                v.tensor_tensor(P.PSB[:, 0:2048].rearrange("p (c n) -> p c n", c=8),
                                C2[:, 2048:4096].rearrange("p (c n) -> p c n", c=8),
                                bass.AP(P.B, 8448, [[9216, 128], [0, 8], [1, 256]]),
                                ALU.mult)
                v.tensor_add(B[:, 4096:6144], B[:, 4096:6144], P.PSA[:, 0:2048])
                P.dv.inc(v.tensor_add(B[:, 6144:8192], B[:, 6144:8192],
                                      P.PSB[:, 0:2048]), 2)
                v.tensor_mul(B[:, 4096:8192], B[:, 4096:8192], B[:, 0:4096])
                P.dv.inc(v.tensor_scalar_mul(B[:, 0:4096], B[:, 0:4096], -1.0))
                for c in range(NCH):
                    ins = v.tensor_tensor_scan(
                        B[:, c * 256:(c + 1) * 256], B[:, c * 256:(c + 1) * 256],
                        CST[:, 0:256], 0.0, ALU.add, ALU.max)
                P.dv.inc(ins)
                v.wait_ge(P.ac.h, ac0 + 2)
                P.dv.inc(v.tensor_mul(B[:, 4096:8192], B[:, 4096:8192],
                                      C[:, 4096:8192]), 2)
                for c in range(NCH):
                    ins = v.tensor_tensor_scan(
                        B[:, 4096 + c * 256:4096 + (c + 1) * 256],
                        B[:, 4096 + c * 256:4096 + (c + 1) * 256],
                        CST[:, 256:512], 0.0, ALU.add, ALU.add)
                P.dv.inc(ins)
                P.dv.inc(v.tensor_tensor(C2[:, 0:4096], C[:, 0:4096],
                                         B[:, 4096:8192], ALU.mult))
                for c in range(NCH):
                    v.tensor_copy(BDS[:, c * 8 + q:c * 8 + q + 1],
                                  B[:, c * 256 + 255:c * 256 + 256])
                    ins = v.tensor_copy(BOS[:, c * 8 + q:c * 8 + q + 1],
                                        C2[:, c * 256 + 255:c * 256 + 256])
                P.dv.inc(ins, 32)

            @blk.scalar
            def _(s):
                s.wait_ge(P.dv.h, dv0 + 4)
                P.ac.inc(s.activation(C[:, 0:4096], B[:, 0:4096], AF.Exp))
                P.ac.inc(s.activation(C[:, 4096:8192], B[:, 0:4096], AF.Exp,
                                      scale=-1.0))

    with nc.Block("t1_cb") as blk:
        dv0, po0, ac0 = P.dv.n, P.po.n, P.ac.n

        @blk.gpsimd
        def _(g):
            g.reg_add(P.rg_a, P.rg_e, XO_BD)
            g.reg_alu(P.rg_a, P.rg_a, P.rg_d, ALU.add)
            P.so.inc(g.dma_start(
                bass.AP(io["l1a_xch"], P.rg_a, [[256, 128], [16, 16], [1, 8]]),
                BDS[:, :].rearrange("p (c n) -> p c n", c=16),
                bounds_check=None), 16)
            g.reg_add(P.rg_a, P.rg_e, XO_BO)
            g.reg_alu(P.rg_a, P.rg_a, P.rg_d, ALU.add)
            P.so.inc(g.dma_start(
                bass.AP(io["l1a_xch"], P.rg_a, [[256, 128], [16, 16], [1, 8]]),
                BOS[:, :].rearrange("p (c n) -> p c n", c=16),
                bounds_check=None), 16)
            g.wait_ge(P.so.h, P.so.n)
            g.reg_add(P.rg_a, P.rg_b, XO_FL2)
            P.so.inc(g.dma_start(
                bass.AP(io["l1a_xch"], P.rg_a, [[2, 1], [1, 2]]),
                NON[:, :], bounds_check=None), 16)
            g.wait_ge(P.so.h, P.so.n)
            g.reg_add(P.rg_a, P.rg_c, XO_FL2)
            with g.register("t2w") as wt, g.register("t2c") as cond, \
                    g.register("t2l") as flo, g.register("t2h") as fhi, \
                    g.register("t2nl") as nlo, g.register("t2nh") as nhi:
                g.reg_load(nlo, NON[0:1, 0:1])
                g.reg_load(nhi, NON[0:1, 1:2])
                g.reg_mov(wt, P.po.n)
                g.reg_mov(cond, 1)
                with g.While(cond):
                    g.dma_start(FLG[:, :],
                                bass.AP(io["l1a_xch"], P.rg_a, [[2, 1], [1, 2]]),
                                bounds_check=None).then_inc(P.po.h, 16)
                    g.reg_add(wt, wt, 16)
                    g.wait_ge(P.po.h, wt)
                    g.reg_load(flo, FLG[0:1, 0:1])
                    g.reg_load(fhi, FLG[0:1, 1:2])
                    g.reg_alu(flo, flo, nlo, ALU.not_equal)
                    g.reg_alu(fhi, fhi, nhi, ALU.not_equal)
                    g.reg_alu(cond, flo, fhi, ALU.bitwise_or)
            g.reg_add(P.rg_a, P.rg_e, XO_BD)
            g.dma_start(B[:, 0:256],
                        bass.AP(io["l1a_xch"], P.rg_a, [[256, 128], [1, 256]]),
                        bounds_check=None).then_inc(P.po.h, 16)
            g.reg_add(P.rg_a, P.rg_e, XO_BO)
            g.dma_start(B[:, 256:512],
                        bass.AP(io["l1a_xch"], P.rg_a, [[256, 128], [1, 256]]),
                        bounds_check=None).then_inc(P.po.h, 16)
            P.po.n += 32
            g.wait_ge(P.po.h, P.po.n)

        @blk.vector
        def _(v):
            v.wait_ge(P.po.h, P.po.n)

            def ss(ins):
                P.dv.inc(ins)
                v.wait_ge(P.dv.h, P.dv.n)

            for c in range(NCH):
                ss(v.tensor_tensor_scan(B[:, c * 16:(c + 1) * 16],
                                        B[:, c * 16:(c + 1) * 16],
                                        CST[:, 512:528], 0.0, ALU.add, ALU.max))
            for c in range(NCH):
                ss(v.tensor_scalar(B[:, c * 16:(c + 1) * 16], B[:, c * 16:(c + 1) * 16],
                                   B[:, c * 16:c * 16 + 1], -20.0,
                                   ALU.subtract, ALU.max))
            v.wait_ge(P.ac.h, ac0 + 1)
            ss(v.tensor_mul(B[:, 256:512], B[:, 256:512], B[:, 512:768]))
            ss(v.tensor_copy(B[:, 768:1024], B[:, 256:512]))
            for c in range(NCH):
                ss(v.tensor_tensor_scan(B[:, 256 + c * 16:256 + (c + 1) * 16],
                                        B[:, 256 + c * 16:256 + (c + 1) * 16],
                                        CST[:, 256:272], 0.0, ALU.add, ALU.add))
            ss(v.tensor_sub(B[:, 256:512], B[:, 256:512], B[:, 768:1024]))
            v.wait_ge(P.ac.h, ac0 + 2)
            ss(v.tensor_mul(B[:, 256:512], B[:, 256:512], B[:, 512:768]))

        @blk.scalar
        def _(s):
            s.wait_ge(P.dv.h, dv0 + 32)
            P.ac.inc(s.activation(B[:, 512:768], B[:, 0:256], AF.Exp, scale=-1.0))
            s.wait_ge(P.dv.h, dv0 + 32 + 18)
            P.ac.inc(s.activation(B[:, 512:768], B[:, 0:256], AF.Exp))

        @blk.gpsimd
        def _(g):
            g.wait_ge(P.dv.h, P.dv.n)
            g.dma_start(io["l1a_car"][:, :], B[:, 256:512]).then_inc(P.po.h, 16)
            P.po.n += 16
            g.wait_ge(P.po.h, P.po.n)
            g.dma_start(BDS[:, :].rearrange("p (c n) -> p c n", c=16),
                        bass.AP(io["l1a_car"], P.rg_d, [[256, 128], [16, 16], [1, 8]]),
                        bounds_check=None).then_inc(P.po.h, 16)
            P.po.n += 16
            g.wait_ge(P.po.h, P.po.n)

    for q in range(8):
        with nc.Block(f"t1_fa{q}") as blk:
            ld0, dv0 = P.ld.n, P.dv.n

            @blk.gpsimd
            def _(g, q=q):
                P.ld.inc(g.dma_start(
                    C[:, 0:4096].rearrange("p (c n) -> p c n", c=16),
                    io["l1a_E"].rearrange("(c p) t -> p c t", p=128)[
                        :, :, q * 256:(q + 1) * 256]), 16)
                P.ld.inc(g.dma_start(
                    C[:, 4096:8192].rearrange("p (c n) -> p c n", c=16),
                    io["l1a_CO"].rearrange("(c p) t -> p c t", p=128)[
                        :, :, q * 256:(q + 1) * 256]), 16)

            @blk.vector
            def _(v, q=q):
                v.wait_ge(P.ld.h, P.ld.n)
                for c in range(NCH):
                    P.dv.inc(v.scalar_tensor_tensor(
                        X[:, c * TC + q * 256:c * TC + q * 256 + 256],
                        C[:, c * 256:(c + 1) * 256],
                        BDS[:, c * 8 + q:c * 8 + q + 1],
                        C[:, 4096 + c * 256:4096 + (c + 1) * 256],
                        ALU.mult, ALU.add))

    with nc.Block("t1_hm") as blk:
        ld0, ld20, pe0, dv0 = P.ld.n, P.ld2.n, P.pe.n, P.dv.n

        @blk.gpsimd
        def _(g):
            for m in range(16):
                if m >= 4:
                    g.wait_ge(P.pe.h, pe0 + 4 * (m - 3))
                P.ld.inc(g.dma_start(C[:, (m % 4) * 2048:(m % 4) * 2048 + 2048],
                                     io["l1a_hm"][m]), 16)

        @blk.sync
        def _(sy):
            for m in range(16):
                for tt in range(4):
                    u = 4 * m + tt
                    if u >= 8:
                        sy.wait_ge(P.dv.h, dv0 + u - 7)
                    P.ld2.inc(sy.dma_start(
                        C2[:, (u % 8) * 512:(u % 8) * 512 + 512],
                        io["l1a_gt"][m * 128:(m + 1) * 128, tt * 512:(tt + 1) * 512]), 16)

        @blk.tensor
        def _(t):
            for m in range(16):
                if m >= 2:
                    t.wait_ge(P.dv.h, dv0 + 4 * (m - 1))
                t.wait_ge(P.ld.h, ld0 + 16 * (m + 1))
                ps = P.PSA if m % 2 == 0 else P.PSB
                for tt in range(4):
                    for k in range(NCH):
                        mm = t.matmul(ps[:, tt * 512:tt * 512 + 512],
                                      C[:, (m % 4) * 2048 + k * 128:(m % 4) * 2048 + k * 128 + 128],
                                      X[:, k * TC + tt * 512:k * TC + tt * 512 + 512],
                                      start=(k == 0), stop=(k == NCH - 1))
                    P.pe.inc(mm)

        @blk.vector
        def _(v):
            for m in range(16):
                ps = P.PSA if m % 2 == 0 else P.PSB
                for tt in range(4):
                    u = 4 * m + tt
                    v.wait_ge(P.pe.h, pe0 + u + 1)
                    v.wait_ge(P.ld2.h, ld20 + 16 * (u + 1))
                    P.dv.inc(v.tensor_tensor(
                        A[:, m * TC + tt * 512:m * TC + tt * 512 + 512],
                        ps[:, tt * 512:tt * 512 + 512],
                        C2[:, (u % 8) * 512:(u % 8) * 512 + 512], ALU.mult))

    with nc.Block("t1_op") as blk:
        ld0, ld20, pe0, dv0, so0 = P.ld.n, P.ld2.n, P.pe.n, P.dv.n, P.so.n

        @blk.gpsimd
        def _(g):
            for m in range(16):
                if m >= 4:
                    g.wait_ge(P.pe.h, pe0 + 4 * (m - 3))
                P.ld.inc(g.dma_start(C[:, (m % 4) * 2048:(m % 4) * 2048 + 2048],
                                     io["l1a_opw"][m]), 16)
                if m >= 2:
                    g.wait_ge(P.dv.h, dv0 + 4 * (m - 1))
                P.ld.inc(g.dma_start(C2[:, (m % 2) * 2048:(m % 2) * 2048 + 2048],
                                     io["l1a_h"][m * 128:(m + 1) * 128, :]), 16)
            g.wait_ge(P.so.h, P.so.n)

        @blk.sync
        def _(sy):
            for m in range(16):
                if m >= 2:
                    sy.wait_ge(P.dv.h, dv0 + 4 * (m - 1))
                P.ld2.inc(sy.dma_start(B[:, (m % 2) * 2048:(m % 2) * 2048 + 2048],
                                       xin[m * 128:(m + 1) * 128, :]), 16)
                sy.wait_ge(P.dv.h, dv0 + 4 * m)
                if m >= 1:
                    for tt in range(4):
                        sy.wait_ge(P.dv.h, dv0 + 4 * (m - 1) + tt + 1)
                        P.so.inc(sy.dma_start(
                            xout[(m - 1) * 128:m * 128, tt * 512:(tt + 1) * 512],
                            B[:, 4096 + ((m - 1) % 2) * 2048 + tt * 512:
                              4096 + ((m - 1) % 2) * 2048 + tt * 512 + 512]), 16)
            for tt in range(4):
                sy.wait_ge(P.dv.h, dv0 + 60 + tt + 1)
                P.so.inc(sy.dma_start(
                    xout[15 * 128:16 * 128, tt * 512:(tt + 1) * 512],
                    B[:, 4096 + 1 * 2048 + tt * 512:4096 + 1 * 2048 + tt * 512 + 512]), 16)
            sy.wait_ge(P.so.h, P.so.n)

        @blk.tensor
        def _(t):
            for m in range(16):
                if m >= 2:
                    t.wait_ge(P.dv.h, dv0 + 4 * (m - 1))
                t.wait_ge(P.ld.h, ld0 + 32 * m + 16)
                ps = P.PSA if m % 2 == 0 else P.PSB
                for tt in range(4):
                    for k in range(NCH):
                        mm = t.matmul(ps[:, tt * 512:tt * 512 + 512],
                                      C[:, (m % 4) * 2048 + k * 128:(m % 4) * 2048 + k * 128 + 128],
                                      A[:, k * TC + tt * 512:k * TC + tt * 512 + 512],
                                      start=(k == 0), stop=(k == NCH - 1))
                    P.pe.inc(mm)

        @blk.vector
        def _(v):
            for m in range(16):
                ps = P.PSA if m % 2 == 0 else P.PSB
                v.wait_ge(P.ld.h, ld0 + 32 * (m + 1))
                v.wait_ge(P.ld2.h, ld20 + 16 * (m + 1))
                if m >= 2:
                    v.wait_ge(P.so.h, so0 + 64 * (m - 1))
                for tt in range(4):
                    u = 4 * m + tt
                    v.wait_ge(P.pe.h, pe0 + u + 1)
                    out = B[:, 4096 + (m % 2) * 2048 + tt * 512:
                            4096 + (m % 2) * 2048 + tt * 512 + 512]
                    v.tensor_tensor(out, ps[:, tt * 512:tt * 512 + 512],
                                    B[:, (m % 2) * 2048 + tt * 512:(m % 2) * 2048 + tt * 512 + 512],
                                    ALU.add)
                    P.dv.inc(v.tensor_tensor(
                        out, out,
                        C2[:, (m % 2) * 2048 + tt * 512:(m % 2) * 2048 + tt * 512 + 512],
                        ALU.add))

    io["_BDS"] = BDS
    io["_BOS"] = BOS


def run_l1attn(x_full, inputs):
    with contextlib.ExitStack() as st:
        nc = bass.Bass("TRN2", target_bir_lowering=False)
        xin = nc.dram_tensor("xin", [D, TC], F32, kind="ExternalInput")
        xout = nc.dram_tensor("xout", [D, TC], F32, kind="ExternalOutput")
        io = decl_l1attn_io(nc)
        P = Prog(nc, st)
        P.emit_consts()
        build_l1attn(P, io, xin, xout)
        rng = np.random.default_rng()
        nonce = np.array([[rng.integers(1, 2**24), rng.integers(1, 2**24)]],
                         dtype=np.uint32)
        shards = _shard_x(x_full)
        shared = prep_l1attn_inputs(inputs, 0, nonce)
        in_maps = []
        for c in range(NCORE):
            m = prep_l1attn_inputs(inputs, c, nonce)
            for k2 in ("l1a_ipw", "l1a_hm", "l1a_opw", "l1a_ns", "l1a_cw",
                       "l1a_cb", "l1a_dtb"):
                m[k2] = shared[k2]
            m["xin"] = shards[c]
            in_maps.append(m)
        res = run_bass_kernel_spmd(nc, in_maps, core_ids=list(range(NCORE)))
        return _unshard_x([res.results[c]["xout"] for c in range(NCORE)])
